# version 02
# baseline (speedup 1.0000x reference)
import math
import numpy as np
import ml_dtypes
import concourse.bass as bass
import concourse.mybir as mybir
from concourse.bass_utils import run_bass_kernel_spmd

F32 = mybir.dt.float32
BF16 = mybir.dt.bfloat16
U8 = mybir.dt.uint8
AF = mybir.ActivationFunctionType
ALU = mybir.AluOpType
AX = mybir.AxisListType
BFD = ml_dtypes.bfloat16

B, S, D, H, HD = 2, 4096, 4096, 32, 128
NT, NTT = 1024, 8
N_CORES = 8
EPS_LN = 1e-5
EPS_RMS = 1.1920929e-07
QSCALE = float(np.float32(1.0) / np.float32(math.sqrt(HD)))

_LIMIT = "all"
_DBG = False
_FFN_PART = 1000


def _slopes():
    m = np.arange(1, 33, dtype=np.float32) * np.float32(0.25)
    return (np.float32(1.0) / np.power(np.float32(2.0), m)).astype(np.float32)

SLOPES = _slopes()
ALIBI = SLOPES[:, None] * np.arange(1 - S, 1, dtype=np.float32)[None, :]
QS_H = [64 if h < 4 else 128 for h in range(H)]
NQB_H = [2 if h < 4 else 1 for h in range(H)]
WCH = [max(1, int(math.ceil(math.ceil(96.0 / float(SLOPES[h])) / 128.0)))
       for h in range(H)]

def c_lo(h, t):
    return max(0, 4 * t - WCH[h])

ATT_UNITS = []
EA_OFF = {}
_nea = 0
for _h in range(H):
    for _t in range(NTT):
        for _qb in range(NQB_H[_h]):
            ch = list(range(c_lo(_h, _t), 4 * _t + 4))
            ATT_UNITS.append((_h, _t, _qb, ch))
            EA_OFF[(_h, _t, _qb)] = _nea
            _nea += len(ch)
NEA = _nea
assert NEA <= 4096, NEA

def ea_table(ci):
    tab = np.zeros((128, NEA), dtype=np.float32)
    p = np.arange(128)
    for (h, t, qb, ch) in ATT_UNITS:
        qs = QS_H[h]
        q_last = 128 * (4 * t + ci) + (qb + 1) * qs - 1
        C = ALIBI[h, q_last]
        off = EA_OFF[(h, t, qb)]
        for i, c in enumerate(ch):
            a = ALIBI[h, 128 * c + p]
            tab[:, off + i] = np.exp(np.minimum(a - C, np.float32(0.0)))
    return tab

def mask_table(ci):
    tab = np.zeros((128, 1024), dtype=np.float32)
    tk = np.arange(128)[:, None]
    j128 = np.arange(128)[None, :]
    j64 = np.arange(64)[None, :]
    for rel in range(4):
        tab[:, rel * 128:(rel + 1) * 128] = (tk + 128 * (rel - ci) <= j128)
        for qb in range(2):
            base = 512 + (rel * 2 + qb) * 64
            tab[:, base:base + 64] = (tk + 128 * (rel - ci) - 64 * qb <= j64)
    return tab

def mask_off(rel, qs, qb):
    if rel < 0:
        return None
    return rel * 128 if qs == 128 else 512 + (rel * 2 + qb) * 64


def prep_inputs(inp):
    x = np.asarray(inp["L0_attn_x"], dtype=np.float32)
    w_qkv = np.asarray(inp["L0_attn_w_qkv"], dtype=np.float32)
    b_qkv = np.asarray(inp["L0_attn_b_qkv"], dtype=np.float32)
    w_out = np.asarray(inp["L0_attn_w_out"], dtype=np.float32)
    b_out = np.asarray(inp["L0_attn_b_out"], dtype=np.float32)
    W_res = np.asarray(inp["L0_ffn_W_res"], dtype=np.float32)
    W_pre = np.asarray(inp["L0_ffn_W_pre"], dtype=np.float32)
    W_post = np.asarray(inp["L0_ffn_W_post"], dtype=np.float32)
    H_res_b = np.asarray(inp["L0_ffn_H_res_base"], dtype=np.float32)
    H_pre_b = np.asarray(inp["L0_ffn_H_pre_base"], dtype=np.float32)
    H_post_b = np.asarray(inp["L0_ffn_H_post_base"], dtype=np.float32)
    gate = np.asarray(inp["L0_ffn_gate"], dtype=np.float32)
    W_branch = np.asarray(inp["L0_ffn_W_branch"], dtype=np.float32)

    wqkvT = np.ascontiguousarray(w_qkv.T)
    wqk = wqkvT[:, :8192].reshape(32, 128, 64, 128).transpose(2, 1, 0, 3)
    wqk_tiled = np.ascontiguousarray(wqk).reshape(64, 128, 4096)
    wv = wqkvT[:, 8192:].reshape(32, 128, 32, 128).transpose(2, 1, 0, 3)
    wv_tiled = np.ascontiguousarray(wv).reshape(32, 128, 4096)
    woT = np.ascontiguousarray(w_out.T)
    wo = woT.reshape(32, 128, 32, 128).transpose(2, 1, 0, 3)
    wout_tiled = np.ascontiguousarray(wo).reshape(32, 128, 4096)
    wbr = W_branch.reshape(8, 128, 8, 128).transpose(1, 2, 0, 3)
    wbr_tiled = np.ascontiguousarray(wbr).reshape(128, 8192)
    wsm = np.concatenate([W_res.T, W_pre.T, W_post.T], axis=1)
    wsm_tiled = np.ascontiguousarray(
        wsm.reshape(32, 128, 24).transpose(1, 0, 2)).reshape(128, 768)
    bias_pf = np.ascontiguousarray(b_qkv.reshape(96, 128).T)
    hsmall = np.concatenate([H_res_b.reshape(-1), H_pre_b, H_post_b, gate])
    shared = dict(
        wqk_tiled=wqk_tiled, wv_tiled=wv_tiled, wout_tiled=wout_tiled,
        wbr_tiled=wbr_tiled, wsm_tiled=wsm_tiled, bias_pf=bias_pf,
        b_v=np.ascontiguousarray(b_qkv[8192:]).reshape(8, 512),
        b_out=np.ascontiguousarray(b_out).reshape(8, 512),
        hsmall=hsmall.astype(np.float32).reshape(1, 25),
    )
    per_core = []
    for c in range(N_CORES):
        b, ci = c // 4, c % 4
        xs = np.concatenate(
            [x[b, 128 * (4 * t + ci):128 * (4 * t + ci) + 128, :]
             for t in range(NTT)], axis=0)
        per_core.append(dict(x_core=np.ascontiguousarray(xs),
                             masks01=mask_table(ci), ea_tab=ea_table(ci)))
    return shared, per_core


ENGS = ("sync", "gpsimd", "tensor", "vector", "scalar")

class Tape:
    def __init__(self, nc):
        self.nc = nc
        self.ops = {e: [] for e in ENGS}
        self.sems = {}
        self.cnt = {}

    def sem(self, name):
        if name not in self.sems:
            self.sems[name] = self.nc.alloc_semaphore(name)
            self.cnt[name] = 0
        return self.sems[name]

    def emit(self, eng, fn, waits=(), incs=()):
        rw = [(self.sem(s), int(val)) for s, val in waits if int(val) > 0]
        ri = []
        for s, n in incs:
            ri.append((self.sem(s), n))
            self.cnt[s] += n
        self.ops[eng].append((rw, fn, ri))

    def wait(self, eng, name, val):
        if int(val) > 0:
            self.ops[eng].append(([(self.sem(name), int(val))], None, []))

    def v(self, name):
        self.sem(name)
        return self.cnt[name]

    def replay(self, eng, engine):
        for rw, fn, ri in self.ops[eng]:
            for s, val in rw:
                engine.wait_ge(s, val)
            if fn is None:
                continue
            ins = fn(engine)
            for s, n in ri:
                ins.then_inc(s, n)


class Ctx:
    pass


def build_program():
    nc = bass.Bass("TRN2", target_bir_lowering=False,
                   dynamic_dma_scratch_size=4096)
    c = Ctx()
    c.nc = nc

    c.x_d = nc.dram_tensor("x_core", [NT, D], F32, kind="ExternalInput").ap()
    c.wqk_d = nc.dram_tensor("wqk_tiled", [64, 128, 4096], F32, kind="ExternalInput").ap()
    c.wv_d = nc.dram_tensor("wv_tiled", [32, 128, 4096], F32, kind="ExternalInput").ap()
    c.wout_d = nc.dram_tensor("wout_tiled", [32, 128, 4096], F32, kind="ExternalInput").ap()
    c.wbr_d = nc.dram_tensor("wbr_tiled", [128, 8192], F32, kind="ExternalInput").ap()
    c.wsm_d = nc.dram_tensor("wsm_tiled", [128, 768], F32, kind="ExternalInput").ap()
    c.bias_d = nc.dram_tensor("bias_pf", [128, 96], F32, kind="ExternalInput").ap()
    c.bv_d = nc.dram_tensor("b_v", [8, 512], F32, kind="ExternalInput").ap()
    c.bo_d = nc.dram_tensor("b_out", [8, 512], F32, kind="ExternalInput").ap()
    c.hs_d = nc.dram_tensor("hsmall", [1, 25], F32, kind="ExternalInput").ap()
    c.mk_d = nc.dram_tensor("masks01", [128, 1024], F32, kind="ExternalInput").ap()
    c.ea_d = nc.dram_tensor("ea_tab", [128, NEA], F32, kind="ExternalInput").ap()
    c.out_d = nc.dram_tensor("out_core", [NT, D], F32, kind="ExternalOutput").ap()

    c.qdr = nc.dram_tensor("q_scr", [4096, 1024], BF16)
    c.cck_i = nc.dram_tensor("cc_in_k", [4096, 1024], BF16)
    c.ccv_i = nc.dram_tensor("cc_in_v", [H * 1024, 128], BF16)
    c.cck_o = nc.dram_tensor("cc_out_k", [4 * 4096, 1024], BF16)
    c.ccv_o = nc.dram_tensor("cc_out_v", [4 * H * 1024, 128], BF16)

    c.ident_d = nc.inline_tensor(np.eye(128, dtype=BFD), "ident_d")
    c.i24_d = nc.inline_tensor(np.eye(24, dtype=np.float32), "i24_d")
    c.onesf_d = nc.inline_tensor(np.ones((1, 128), dtype=np.float32), "onesf_d")
    c.onesb_d = nc.inline_tensor(np.ones((128, 1), dtype=BFD), "onesb_d")
    c.eps_d = nc.inline_tensor(
        np.concatenate([np.full((128, 1), EPS_LN, dtype=np.float32),
                        np.full((128, 1), EPS_RMS, dtype=np.float32)], axis=1),
        "eps_d")

    if _DBG:
        c.dbg_bigB = nc.dram_tensor("dbg_bigB", [128, 8 * D], BF16, kind="ExternalOutput").ap()
        c.dbg_bigA = nc.dram_tensor("dbg_bigA", [128, 32 * NT], BF16, kind="ExternalOutput").ap()
        c.dbg_f32 = nc.dram_tensor("dbg_f32", [128, 1024], F32, kind="ExternalOutput").ap()
        c.dbg_q = nc.dram_tensor("dbg_q", [4096, 1024], BF16, kind="ExternalOutput").ap()
        c.dbg_k = nc.dram_tensor("dbg_k", [4096, 1024], BF16, kind="ExternalOutput").ap()
        c.dbg_v = nc.dram_tensor("dbg_v", [H * 1024, 128], BF16, kind="ExternalOutput").ap()

    def arena(name, nbytes):
        return nc.alloc_sbuf_tensor(name, [128, nbytes], U8)[:]
    R2 = arena("R2", 16384)
    R3 = arena("R3", 16384)
    R4 = arena("R4", 16384)
    R5 = arena("R5", 8192)
    R7 = arena("R7", 8192)
    R8 = arena("R8", 4096)

    c.xt = R2[:, 0:16384].bitcast(F32)
    c.wbr_bf = R2[:, 0:16384].bitcast(BF16)
    c.ea_f = R2[:, 0:4 * NEA].bitcast(F32)
    c.band = [R3[:, 0:8192].bitcast(BF16), R4[:, 0:8192].bitcast(BF16)]
    c.xpreT = R3[:, 0:16384].bitcast(BF16)
    c.juggle = R4[:, 0:16384].bitcast(BF16)
    c.kun = [R3[:, 0:8192].bitcast(BF16), R4[:, 0:8192].bitcast(BF16)]
    c.vun = [R3[:, 8192:16384].bitcast(BF16), R4[:, 8192:16384].bitcast(BF16)]
    c.ytok = R3[:, 0:16384].bitcast(BF16)
    c.scrbf = R5[:, 0:8192].bitcast(BF16)
    c.ering = [R5[:, 0:1024].bitcast(BF16), R5[:, 1024:2048].bitcast(BF16)]
    c.pring = [R5[:, 2048:3072].bitcast(BF16), R5[:, 3072:4096].bitcast(BF16)]
    c.rbsb = R5[:, 4096:6144].bitcast(F32)
    c.rlq = R5[:, 6144:8192].bitcast(F32)
    c.lgf = R5[:, 0:4096].bitcast(F32)[0:24, :]
    c.rowb = R5[0:1, 4096:6144].bitcast(F32)
    c.stage = [R8[:, 0:1024].bitcast(BF16), R8[:, 1024:2048].bitcast(BF16)]
    c.xpre = R8[:, 0:4096].bitcast(F32)
    c.bvb = R7[:, 0:8192].bitcast(BF16)
    c.bob = R7[:, 0:8192].bitcast(BF16)

    al = nc.alloc_sbuf_tensor
    c.bigA = al("bigA", [128, 32 * NT], BF16)[:]
    c.bigB = al("bigB", [128, 8 * D], BF16)[:]
    c.masks = al("masks", [128, 1024], BF16)[:]
    c.onesb = al("onesb", [128, 1], BF16)[:]
    c.ident = al("ident", [128, 128], BF16)[:]
    c.i24 = al("i24", [24, 24], F32)[:]
    c.onesf = al("onesf", [1, 128], F32)[:]
    c.bias_sb = al("bias_sb", [128, 96], F32)[:]
    c.bias_q = al("bias_q", [128, 32], F32)[:]
    c.hs = al("hs", [1, 32], F32)[:]
    c.base24 = al("base24", [128, 24], F32)[:]
    c.wsm_bf = al("wsm_bf", [128, 768], BF16)[:]
    c.lgraw = al("lgraw", [128, 192], F32)[:]
    c.lgs = al("lgs", [128, 192], F32)[:]
    c.zbuf = al("zbuf", [128, 128], F32)[:]
    c.abuf = al("abuf", [128, 128], F32)[:]
    c.redu = al("redu", [128, 32], F32)[:]
    c.lnsu = al("lnsu", [128, 32], F32)[:]
    c.lnsv = al("lnsv", [128, 32], F32)[:]
    c.hres = al("hres", [128, 128], F32)[:]
    c.hpre = al("hpre", [128, 32], F32)[:]
    c.hpost = al("hpost", [128, 32], F32)[:]
    c.gcol = al("gcol", [128, 2], F32)[:]
    c.epsc = al("epsc", [128, 2], F32)[:]
    c.g1 = al("g1", [1, 3], F32)[:]
    c.stat = al("stat", [128, 16], F32)[:]
    c.rms_r = al("rms_r", [128, 8], F32)[:]
    c.qh = [al(f"qh{i}", [128, 1024], BF16)[:] for i in range(2)]
    c.ot1 = al("ot1", [128, 128], F32)[:]
    c.ot2 = al("ot2", [128, 128], F32)[:]
    c.xcol = [al(f"xcol{i}", [128, 128], F32)[:] for i in range(2)]
    c.ostg = [al(f"ostg{i}", [128, 128], F32)[:] for i in range(2)]

    c.ps = nc.alloc_psum_tensor("pspool", [128, 4096], F32)[:]
    c.bank = lambda i, w=512, r0=0, rn=128: c.ps[r0:r0 + rn, 512 * i:512 * i + w]

    T = Tape(nc)
    c.T = T

    c.g_seq = 0
    c.evac_info = []

    _emit_preload(c)
    _emit_p0(c)
    _emit_bvb(c)
    if _LIMIT != "p0":
        _emit_p1(c)
    if _LIMIT not in ("p0", "p1"):
        _emit_ffn(c)
    if _LIMIT not in ("p0", "p1", "ffn"):
        _emit_attention(c)
    if _LIMIT == "all":
        _emit_wout(c)
    if _DBG:
        _emit_debug(c)

    T.wait("sync", "s_out0", T.v("s_out0"))
    T.wait("sync", "s_out1", T.v("s_out1"))
    T.wait("sync", "s_outd", T.v("s_outd"))

    with nc.Block() as block:
        @block.sync
        def _(sy):
            T.replay("sync", sy)

        @block.gpsimd
        def _(g):
            T.replay("gpsimd", g)

        @block.tensor
        def _(t):
            T.replay("tensor", t)

        @block.vector
        def _(v):
            T.replay("vector", v)

        @block.scalar
        def _(s):
            T.replay("scalar", s)
    return nc


def _emit_preload(c):
    T = c.T
    for dst, srcap in [(c.onesb, c.onesb_d.ap()), (c.epsc, c.eps_d.ap()),
                       (c.ident, c.ident_d.ap()), (c.i24, c.i24_d.ap()),
                       (c.onesf, c.onesf_d.ap()), (c.bias_sb, c.bias_d),
                       (c.hs[0:1, 0:25], c.hs_d)]:
        T.emit("sync", lambda e, d=dst, s=srcap: e.dma_start(out=d, in_=s),
               incs=(("s_pre", 16),))
    T.emit("gpsimd", lambda e: e.dma_start(out=c.masks, in_=c.mk_d),
           incs=(("s_mk", 16),))
    c.PRE = T.v("s_pre")
    T.emit("vector", lambda e: e.tensor_scalar_mul(c.bias_q, c.bias_sb[:, 0:32], QSCALE),
           waits=(("s_pre", c.PRE),), incs=(("s_bq", 1),))


def _emit_bvb(c):
    T = c.T
    for j in range(8):
        T.emit("sync", lambda e, j=j: e.dma_start(out=c.rowb, in_=c.bv_d[j:j + 1, :]),
               waits=(("s_bvb", j),), incs=(("s_row", 16),))
        T.emit("tensor", lambda e, j=j: e.matmul(
            c.bank(6 + j % 2), c.onesf, c.rowb, start=True, stop=True,
            skip_group_check=True),
            waits=(("s_row", 16 * (j + 1)), ("s_bvb", j)), incs=(("s_bvm", 1),))
        T.emit("vector", lambda e, j=j: e.tensor_copy(
            c.bvb[:, 512 * j:512 * (j + 1)], c.bank(6 + j % 2)),
            waits=(("s_bvm", j + 1),), incs=(("s_bvb", 1),))
    c.BVB = T.v("s_bvb")


def _emit_p0(c):
    T = c.T
    ps = c.ps
    tgrp = 0
    for tc in range(NTT):
        w = [("s_xn", tc)] if tc > 0 else []
        T.emit("sync", lambda e, tc=tc: e.dma_start(
            out=c.xt, in_=c.x_d[128 * tc:128 * (tc + 1), :]),
            waits=tuple(w), incs=(("s_x", 16),))
        XV = T.v("s_x")
        T.emit("vector", lambda e: e.reduce_sum(c.stat[:, 0:1], c.xt, AX.X),
               waits=(("s_x", XV),), incs=(("s_sum", 1),))
        T.emit("scalar", lambda e: e.activation(
            ps[:, 1024:3072], c.xt[:, 0:2048], AF.Square, accum_out=c.stat[:, 1:2]),
            waits=(("s_x", XV),), incs=(("s_ta", 1),))
        T.emit("scalar", lambda e: e.activation(
            ps[:, 1024:3072], c.xt[:, 2048:4096], AF.Square, accum_out=c.stat[:, 2:3]),
            waits=(("s_ta", T.v("s_ta")),), incs=(("s_sq", 1),))
        SQ, SM = T.v("s_sq"), T.v("s_sum")

        def vstep(fn, extra=()):
            T.emit("vector", fn, waits=tuple(extra) + (("s_tv", T.v("s_tv")),),
                   incs=(("s_tv", 1),))

        def astep(fn, extra=()):
            T.emit("scalar", fn, waits=tuple(extra) + (("s_ta", T.v("s_ta")),),
                   incs=(("s_ta", 1),))

        vstep(lambda e: e.tensor_add(c.stat[:, 3:4], c.stat[:, 1:2], c.stat[:, 2:3]),
              ((("s_sq"), SQ),))
        vstep(lambda e: e.tensor_scalar_mul(c.stat[:, 4:5], c.stat[:, 0:1], 1.0 / D),
              ((("s_sum"), SM),))
        vstep(lambda e: e.tensor_scalar_mul(c.stat[:, 5:6], c.stat[:, 3:4], 1.0 / D))
        vstep(lambda e: e.tensor_mul(c.stat[:, 6:7], c.stat[:, 4:5], c.stat[:, 4:5]))
        vstep(lambda e: e.tensor_sub(c.stat[:, 7:8], c.stat[:, 5:6], c.stat[:, 6:7]))
        TV = T.v("s_tv")
        astep(lambda e: e.activation(c.stat[:, 8:9], c.stat[:, 7:8], AF.Sqrt,
                                     bias=c.epsc[:, 0:1]), ((("s_tv"), TV),))
        TA = T.v("s_ta")
        vstep(lambda e: e.reciprocal(c.stat[:, 9:10], c.stat[:, 8:9]),
              ((("s_ta"), TA),))
        vstep(lambda e: e.tensor_mul(c.stat[:, 11:12], c.stat[:, 7:8], c.stat[:, 9:10]))
        vstep(lambda e: e.tensor_mul(c.stat[:, 12:13], c.stat[:, 11:12], c.stat[:, 9:10]))
        TV = T.v("s_tv")
        astep(lambda e: e.activation(c.stat[:, 13:14], c.stat[:, 12:13], AF.Sqrt,
                                     bias=c.epsc[:, 1:2]), ((("s_tv"), TV),))
        TA = T.v("s_ta")
        vstep(lambda e, tc=tc: e.reciprocal(c.rms_r[:, tc:tc + 1], c.stat[:, 13:14]),
              ((("s_ta"), TA),))
        TV = T.v("s_tv")
        T.emit("vector", lambda e, tc=tc: e.tensor_scalar(
            c.bigB[:, D * tc:D * (tc + 1)], c.xt, c.stat[:, 4:5], c.stat[:, 9:10],
            ALU.subtract, ALU.mult),
            waits=(("s_tv", TV),), incs=(("s_xn", 1),))
        XN = T.v("s_xn")
        for g in range(8):
            bk = tgrp % 2
            for q in range(4):
                dc = 4 * g + q
                incs = (("s_tmm", 1),) if q == 3 else ()
                T.emit("tensor", lambda e, tc=tc, dc=dc, bk=bk, q=q: e.matmul(
                    c.bank(bk)[:, 128 * q:128 * (q + 1)],
                    c.bigB[:, D * tc + 128 * dc:D * tc + 128 * (dc + 1)], c.ident,
                    start=True, stop=True, skip_group_check=True),
                    waits=(("s_xn", XN), ("s_pre", c.PRE),
                           ("s_tev", max(0, tgrp - 1))),
                    incs=incs)
            tgrp += 1
            TM = T.v("s_tmm")
            T.emit("scalar", lambda e, tc=tc, g=g, bk=bk: e.activation(
                c.bigA.rearrange("p (c t) -> p c t", t=NT)
                    [:, 4 * g:4 * g + 4, 128 * tc:128 * (tc + 1)],
                c.bank(bk).rearrange("p (c t) -> p c t", t=128),
                AF.Identity),
                waits=(("s_tmm", TM),), incs=(("s_tev", 1),))
    c.P0_XN = T.v("s_xn")
    c.P0_TEV = T.v("s_tev")


def gev_now(c):
    w = []
    for s in ("s_geva", "s_gevv"):
        v = c.T.v(s)
        if v:
            w.append((s, v))
    return tuple(w)


def _mm_group(c, n_mm, mk_mm, evac_eng, mk_evac, extra_mm_waits=(),
              extra_evac_waits=()):
    T = c.T
    g = c.g_seq
    bk = g % 4
    c.g_seq += 1
    war = (c.evac_info[g - 4],) if g >= 4 else ()
    for i in range(n_mm):
        incs = (("s_gmm", 1),) if i == n_mm - 1 else ()
        T.emit("tensor", lambda e, i=i, bk=bk: mk_mm(e, i, bk),
               waits=tuple(extra_mm_waits) + war,
               incs=incs)
    GM = T.v("s_gmm")
    esem = "s_geva" if evac_eng == "scalar" else "s_gevv"
    T.emit(evac_eng, lambda e, bk=bk: mk_evac(e, bk),
           waits=tuple(extra_evac_waits) + (("s_gmm", GM),),
           incs=((esem, 1),))
    c.evac_info.append((esem, T.v(esem)))
    return g


def _emit_p1(c):
    T = c.T
    band_plan = ([("K", i) for i in range(32, 64)] +
                 [("V", i) for i in range(32)] +
                 [("Q", i) for i in range(32)])
    groups_after_band = []
    nst = 0
    c.st_lane = [0, 0]
    c.k_store = None
    c.v_store = None
    for bi, (kind, idx) in enumerate(band_plan):
        par = bi % 2
        src = c.wqk_d if kind in ("K", "Q") else c.wv_d
        si = idx if kind != "V" else idx
        w = [] if bi < 2 else [("s_gmm", groups_after_band[bi - 2])]
        wsem = f"s_w{par}"
        T.emit("gpsimd", lambda e, par=par, src=src, si=si: e.dma_start(
            out=c.band[par], in_=src[si]),
            waits=tuple(w), incs=((wsem, 16),))
        WB = T.v(wsem)
        if kind in ("K", "Q"):
            for tb in range(2):
                def mk_mm(e, i, bk, par=par, tb=tb, WB=WB):
                    return e.matmul(
                        c.bank(bk), c.band[par][:, 128 * i:128 * (i + 1)],
                        c.bigA[:, 1024 * i + 512 * tb:1024 * i + 512 * (tb + 1)],
                        start=(i == 0), stop=(i == 31), skip_group_check=True)
                stg = c.stage[nst % 2]
                if kind == "Q":
                    def mk_ev(e, bk, stg=stg, idx=idx):
                        return e.activation(stg, c.bank(bk), AF.Identity,
                                            bias=c.bias_q[:, idx:idx + 1],
                                            scale=QSCALE)
                    ew = [("s_bq", 1)]
                else:
                    def mk_ev(e, bk, stg=stg, idx=idx):
                        return e.activation(stg, c.bank(bk), AF.Identity,
                                            bias=c.bias_sb[:, idx:idx + 1])
                    ew = []
                ew.append((f"s_st{nst % 2}", 16 * c.st_lane[nst % 2]))
                _mm_group(c, 32, mk_mm, "scalar", mk_ev,
                          extra_mm_waits=((wsem, WB), ("s_tev", c.P0_TEV)),
                          extra_evac_waits=tuple(ew))
                EV = c.evac_info[-1]
                dst = c.cck_i if kind == "K" else c.qdr
                r0 = (idx - 32) * 128 if kind == "K" else idx * 128
                T.emit("sync", lambda e, dst=dst, r0=r0, tb=tb, stg=stg: e.dma_start(
                    out=dst.ap()[r0:r0 + 128, 512 * tb:512 * (tb + 1)], in_=stg),
                    waits=(EV,), incs=((f"s_st{nst % 2}", 16),))
                c.st_lane[nst % 2] += 1
                nst += 1
        else:
            for tc in range(NTT):
                def mk_mm(e, i, bk, par=par, tc=tc):
                    return e.matmul(
                        c.bank(bk, w=128),
                        c.bigA[:, 1024 * i + 128 * tc:1024 * i + 128 * (tc + 1)],
                        c.band[par][:, 128 * i:128 * (i + 1)],
                        start=(i == 0), stop=(i == 31), skip_group_check=True)
                stg = c.stage[nst % 2]
                def mk_ev(e, bk, stg=stg, idx=idx):
                    return e.tensor_tensor(stg[:, 0:128], c.bank(bk, w=128),
                                           c.bvb[:, 128 * idx:128 * (idx + 1)],
                                           ALU.add)
                _mm_group(c, 32, mk_mm, "vector", mk_ev,
                          extra_mm_waits=((wsem, WB), ("s_tev", c.P0_TEV)),
                          extra_evac_waits=(("s_bvb", c.BVB),
                                            (f"s_st{nst % 2}", 16 * c.st_lane[nst % 2])))
                EV = c.evac_info[-1]
                r0 = idx * 1024 + tc * 128
                T.emit("sync", lambda e, r0=r0, stg=stg: e.dma_start(
                    out=c.ccv_i.ap()[r0:r0 + 128, :], in_=stg[:, 0:128]),
                    waits=(EV,), incs=((f"s_st{nst % 2}", 16),))
                c.st_lane[nst % 2] += 1
                nst += 1
        groups_after_band.append(c.g_seq)
        if kind == "K" and idx == 63:
            c.k_store = tuple(c.st_lane)
        if kind == "V" and idx == 31:
            c.v_store = tuple(c.st_lane)

    T.emit("gpsimd", lambda e: e.collective_compute(
        "AllGather", ALU.bypass, replica_groups=[[0, 1, 2, 3], [4, 5, 6, 7]],
        ins=[c.cck_i.ap().opt()], outs=[c.cck_o.ap().opt()]),
        waits=(("s_st0", 16 * c.k_store[0]), ("s_st1", 16 * c.k_store[1])),
        incs=(("s_ccK", 1),))
    T.emit("gpsimd", lambda e: e.collective_compute(
        "AllGather", ALU.bypass, replica_groups=[[0, 1, 2, 3], [4, 5, 6, 7]],
        ins=[c.ccv_i.ap().opt()], outs=[c.ccv_o.ap().opt()]),
        waits=(("s_st0", 16 * c.v_store[0]), ("s_st1", 16 * c.v_store[1])),
        incs=(("s_ccV", 1),))

    T.emit("gpsimd", lambda e: e.dma_start(out=c.wsm_bf, in_=c.wsm_d),
           waits=(("s_gmm", groups_after_band[-2]),), incs=(("s_wsm", 16),))
    WSM = 16
    for tb in range(2):
        def mk_mm(e, i, bk, tb=tb):
            return e.matmul(
                c.bank(bk, r0=0, rn=24), c.wsm_bf[:, 24 * i:24 * (i + 1)],
                c.bigA[:, 1024 * i + 512 * tb:1024 * i + 512 * (tb + 1)],
                start=(i == 0), stop=(i == 31), skip_group_check=True)
        def mk_ev(e, bk, tb=tb):
            return e.activation(c.lgf[0:24, 512 * tb:512 * (tb + 1)],
                                c.bank(bk, r0=0, rn=24), AF.Identity)
        _mm_group(c, 32, mk_mm, "scalar", mk_ev,
                  extra_mm_waits=(("s_wsm", WSM),))
    lg_gate = gev_now(c)
    def mk_mm(e, i, bk):
        return e.matmul(
            c.bank(bk)[:, 24 * i:24 * (i + 1)],
            c.lgf[0:24, 128 * i:128 * (i + 1)], c.i24,
            start=True, stop=True, skip_group_check=True)
    def mk_ev(e, bk):
        return e.tensor_copy(c.lgraw, c.bank(bk, w=192))
    _mm_group(c, 8, mk_mm, "vector", mk_ev,
              extra_mm_waits=(("s_pre", c.PRE),) + lg_gate)
    c.LGR_GATE = gev_now(c)
    c.P1_GROUPS = c.g_seq
    c.TOT_ST = tuple(c.st_lane)
    c.BANDS_GMM = groups_after_band[-1]


def _emit_ffn(c):
    T = c.T

    def vstep(fn, extra=()):
        T.emit("vector", fn, waits=tuple(extra) + (("s_tv", T.v("s_tv")),),
               incs=(("s_tv", 1),))

    def astep(fn, extra=()):
        T.emit("scalar", fn, waits=tuple(extra) + (("s_ta", T.v("s_ta")),),
               incs=(("s_ta", 1),))

    _mm_group(c, 1,
              lambda e, i, bk: e.matmul(c.bank(bk, w=24), c.onesf,
                                        c.hs[0:1, 0:24], start=True, stop=True,
                                        skip_group_check=True),
              "vector",
              lambda e, bk: e.tensor_copy(c.base24, c.bank(bk, w=24)),
              extra_mm_waits=(("s_pre", c.PRE),))
    B24_GATE = gev_now(c)
    astep(lambda e: e.activation(c.g1[0:1, 0:1], c.hs[0:1, 24:25], AF.Exp,
                                 scale=-1.0), ((("s_pre"), c.PRE),))
    TA = T.v("s_ta")
    vstep(lambda e: e.tensor_scalar_add(c.g1[0:1, 1:2], c.g1[0:1, 0:1], 1.0),
          ((("s_ta"), TA),))
    vstep(lambda e: e.reciprocal(c.g1[0:1, 0:1], c.g1[0:1, 1:2]))
    vstep(lambda e: e.memset(c.g1[0:1, 2:3], 1.0))
    vstep(lambda e: e.tensor_sub(c.g1[0:1, 1:2], c.g1[0:1, 2:3], c.g1[0:1, 0:1]))
    TV = T.v("s_tv")
    _mm_group(c, 1,
              lambda e, i, bk: e.matmul(c.bank(bk, w=2), c.onesf,
                                        c.g1[0:1, 0:2], start=True, stop=True,
                                        skip_group_check=True),
              "vector",
              lambda e, bk: e.tensor_copy(c.gcol, c.bank(bk, w=2)),
              extra_mm_waits=(("s_tv", TV),))
    GC_GATE = gev_now(c)

    for tc in range(NTT):
        vstep(lambda e, tc=tc: e.scalar_tensor_tensor(
            c.lgs[:, 24 * tc:24 * (tc + 1)], c.lgraw[:, 24 * tc:24 * (tc + 1)],
            c.rms_r[:, tc:tc + 1], c.base24, ALU.mult, ALU.add),
            B24_GATE)

    lgs3 = c.lgs.rearrange("p (a c) -> p a c", c=24)
    for (dst, off) in ((c.hpre, 16), (c.hpost, 20)):
        d3 = dst.rearrange("p (a n) -> p a n", n=4)
        vstep(lambda e, off=off: e.reduce_max(c.redu[:, 0:8], lgs3[:, :, off:off + 4], AX.X))
        vstep(lambda e, off=off, d3=d3: e.tensor_tensor(
            d3, lgs3[:, :, off:off + 4],
            c.redu[:, 0:8].rearrange("p (a o) -> p a o", o=1).broadcast_to([128, 8, 4]),
            ALU.subtract))
        TV = T.v("s_tv")
        astep(lambda e, dst=dst: e.activation(dst, dst, AF.Exp), ((("s_tv"), TV),))
        TA = T.v("s_ta")
        vstep(lambda e, d3=d3: e.reduce_sum(c.redu[:, 8:16], d3, AX.X),
              ((("s_ta"), TA),))
        vstep(lambda e: e.reciprocal(c.redu[:, 16:24], c.redu[:, 8:16]))
        vstep(lambda e, d3=d3: e.tensor_tensor(
            d3, d3,
            c.redu[:, 16:24].rearrange("p (a o) -> p a o", o=1).broadcast_to([128, 8, 4]),
            ALU.mult))
    HPP = T.v("s_tv")
    if _FFN_PART < 2:
        c.MIX = 0
        c.YTK_GATE = ()
        c.FFN_G = c.g_seq
        return

    z3 = c.zbuf.rearrange("p (a i j) -> p a i j", i=4, j=4)
    z3j = c.zbuf.rearrange("p (a i j) -> p a j i", i=4, j=4)
    a3 = c.abuf.rearrange("p (a i j) -> p a i j", i=4, j=4)
    a3j = c.abuf.rearrange("p (a i j) -> p a j i", i=4, j=4)
    z2 = c.zbuf.rearrange("p (a e) -> p a e", e=16)
    r_i = lambda buf: buf.rearrange("p (a i) -> p a i", i=4).unsqueeze(3).broadcast_to([128, 8, 4, 4])
    r_j = lambda buf: buf.rearrange("p (a j) -> p a j", j=4).unsqueeze(2).broadcast_to([128, 8, 4, 4])

    vstep(lambda e: e.tensor_scalar_mul(z2, lgs3[:, :, 0:16], 10.0))
    vstep(lambda e: e.tensor_reduce(c.redu[:, 0:32], z3, AX.X, ALU.max))
    vstep(lambda e: e.tensor_tensor(z3, z3, r_i(c.redu[:, 0:32]), ALU.subtract))
    vstep(lambda e: e.tensor_reduce(c.redu[:, 0:32], z3j, AX.X, ALU.max))
    vstep(lambda e: e.tensor_tensor(z3j, z3j, r_i(c.redu[:, 0:32]), ALU.subtract))
    TA_prev = None
    for it in range(50):
        if it == 0:
            vstep(lambda e: e.tensor_scalar_max(c.abuf, c.zbuf, -80.0))
        else:
            vstep(lambda e: e.tensor_tensor(a3, z3, r_j(c.lnsv), ALU.subtract),
                  ((("s_ta"), TA_prev),))
            vstep(lambda e: e.tensor_scalar_max(c.abuf, c.abuf, -80.0))
        TV = T.v("s_tv")
        astep(lambda e: e.activation(c.abuf, c.abuf, AF.Exp), ((("s_tv"), TV),))
        TA = T.v("s_ta")
        vstep(lambda e: e.tensor_reduce(c.redu[:, 0:32], a3, AX.X, ALU.add),
              ((("s_ta"), TA),))
        TV = T.v("s_tv")
        astep(lambda e: e.activation(c.lnsu, c.redu[:, 0:32], AF.Ln), ((("s_tv"), TV),))
        TA = T.v("s_ta")
        vstep(lambda e: e.tensor_tensor(a3, z3, r_i(c.lnsu), ALU.subtract),
              ((("s_ta"), TA),))
        vstep(lambda e: e.tensor_scalar_max(c.abuf, c.abuf, -80.0))
        TV = T.v("s_tv")
        astep(lambda e: e.activation(c.abuf, c.abuf, AF.Exp), ((("s_tv"), TV),))
        TA = T.v("s_ta")
        vstep(lambda e: e.tensor_reduce(c.redu[:, 0:32], a3j, AX.X, ALU.add),
              ((("s_ta"), TA),))
        TV = T.v("s_tv")
        astep(lambda e: e.activation(c.lnsv, c.redu[:, 0:32], AF.Ln), ((("s_tv"), TV),))
        TA_prev = T.v("s_ta")
    TA = T.v("s_ta")
    vstep(lambda e: e.tensor_tensor(a3, z3, r_i(c.lnsu), ALU.subtract),
          ((("s_ta"), TA),))
    vstep(lambda e: e.tensor_tensor(a3, a3, r_j(c.lnsv), ALU.subtract))
    TV = T.v("s_tv")
    astep(lambda e: e.activation(c.hres, c.abuf, AF.Exp), ((("s_tv"), TV),))
    TA = T.v("s_ta")
    vstep(lambda e: e.tensor_scalar_mul(c.hres, c.hres, c.gcol[:, 0:1]),
          ((("s_ta"), TA),) + GC_GATE)
    h16 = c.hres.rearrange("p (a e) -> p a e", e=16)
    for i in range(4):
        vstep(lambda e, i=i: e.tensor_scalar_add(
            h16[:, :, 5 * i:5 * i + 1], h16[:, :, 5 * i:5 * i + 1], c.gcol[:, 1:2]))
    HRES = T.v("s_tv")
    if _FFN_PART < 25:
        c.MIX = 0
        c.YTK_GATE = ()
        c.FFN_G = c.g_seq
        return

    for tc in range(NTT):
        xb = lambda n, tc=tc: c.bigB[:, D * tc + 1024 * n:D * tc + 1024 * (n + 1)]
        vstep(lambda e, tc=tc, xb=xb: e.tensor_scalar_mul(
            c.xpre, xb(0), c.hpre[:, 4 * tc:4 * tc + 1]),
            ((("s_tv"), HPP), (("s_st0"), 16 * c.TOT_ST[0]),
             (("s_st1"), 16 * c.TOT_ST[1]), (("s_gmm"), c.BANDS_GMM)))
        for n in (1, 2):
            vstep(lambda e, tc=tc, n=n, xb=xb: e.scalar_tensor_tensor(
                c.xpre, xb(n), c.hpre[:, 4 * tc + n:4 * tc + n + 1], c.xpre,
                ALU.mult, ALU.add))
        vstep(lambda e, tc=tc, xb=xb: e.scalar_tensor_tensor(
            c.juggle[:, 1024 * tc:1024 * (tc + 1)], xb(3),
            c.hpre[:, 4 * tc + 3:4 * tc + 4], c.xpre, ALU.mult, ALU.add))
    XPRE = T.v("s_tv")

    def transpose_sweep(src, dst, gate):
        for tc in range(NTT):
            for gq in range(2):
                def mk_mm(e, q, bk, tc=tc, gq=gq, src=src):
                    cc = 4 * gq + q
                    return e.matmul(
                        c.bank(bk)[:, 128 * q:128 * (q + 1)],
                        src[:, 1024 * tc + 128 * cc:1024 * tc + 128 * (cc + 1)],
                        c.ident, start=True, stop=True, skip_group_check=True)
                def mk_ev(e, bk, tc=tc, gq=gq, dst=dst):
                    return e.activation(
                        dst.rearrange("p (c t) -> p c t", t=1024)
                           [:, 4 * gq:4 * gq + 4, 128 * tc:128 * (tc + 1)],
                        c.bank(bk).rearrange("p (c t) -> p c t", t=128),
                        AF.Identity)
                _mm_group(c, 4, mk_mm, "scalar", mk_ev, extra_mm_waits=gate)
        return gev_now(c)

    if _FFN_PART < 26:
        c.MIX = 0
        c.YTK_GATE = ()
        c.FFN_G = c.g_seq
        return
    XPT_GATE = transpose_sweep(c.juggle, c.xpreT, (("s_tv", XPRE),))
    if _FFN_PART < 255.5:
        c.MIX = 0
        c.YTK_GATE = ()
        c.FFN_G = c.g_seq
        return

    T.emit("gpsimd", lambda e: e.dma_start(out=c.wbr_bf, in_=c.wbr_d),
        waits=(("s_xn", c.P0_XN),), incs=(("s_wbr", 16),))
    WBR = 16
    if _FFN_PART < 256.5:
        c.MIX = 0
        c.YTK_GATE = ()
        c.FFN_G = c.g_seq
        return
    for occ in range(8):
        for tb in range(2):
            def mk_mm(e, i, bk, occ=occ, tb=tb):
                return e.matmul(
                    c.bank(bk), c.wbr_bf[:, 1024 * occ + 128 * i:1024 * occ + 128 * (i + 1)],
                    c.xpreT[:, 1024 * i + 512 * tb:1024 * i + 512 * (tb + 1)],
                    start=(i == 0), stop=(i == 7), skip_group_check=True)
            def mk_ev(e, bk, occ=occ, tb=tb):
                return e.activation(
                    c.juggle[:, 1024 * occ + 512 * tb:1024 * occ + 512 * (tb + 1)],
                    c.bank(bk), AF.Identity)
            _mm_group(c, 8, mk_mm, "scalar", mk_ev,
                      extra_mm_waits=(("s_wbr", WBR),) + XPT_GATE)
    YT_GATE = gev_now(c)

    if _FFN_PART < 27:
        c.MIX = 0
        c.YTK_GATE = ()
        c.FFN_G = c.g_seq
        return
    YTK_GATE = transpose_sweep(c.juggle, c.ytok, YT_GATE)
    c.YTK_GATE = YTK_GATE

    if _FFN_PART < 28:
        c.MIX = 0
        c.FFN_G = c.g_seq
        return

    for tc in range(NTT):
        xb = lambda n, tc=tc: c.bigB[:, D * tc + 1024 * n:D * tc + 1024 * (n + 1)]
        hcol = lambda n, j, tc=tc: c.hres[:, 16 * tc + 4 * n + j:16 * tc + 4 * n + j + 1]
        for n in range(4):
            vstep(lambda e, tc=tc, n=n, xb=xb, hcol=hcol: e.tensor_scalar_mul(
                c.xpre, xb(0), hcol(n, 0)), ((("s_tv"), HRES),) + YTK_GATE)
            for j in (1, 2, 3):
                vstep(lambda e, tc=tc, n=n, j=j, xb=xb, hcol=hcol: e.scalar_tensor_tensor(
                    c.xpre, xb(j), hcol(n, j), c.xpre, ALU.mult, ALU.add))
            vstep(lambda e, tc=tc, n=n, hcol=hcol: e.scalar_tensor_tensor(
                c.scrbf[:, 1024 * n:1024 * (n + 1)],
                c.ytok[:, 1024 * tc:1024 * (tc + 1)],
                c.hpost[:, 4 * tc + n:4 * tc + n + 1], c.xpre, ALU.mult, ALU.add))
        T.emit("vector", lambda e, tc=tc: e.tensor_copy(
            c.bigB[:, D * tc:D * (tc + 1)], c.scrbf),
            waits=(("s_tv", T.v("s_tv")),), incs=(("s_mix", 1),))
    c.MIX = T.v("s_mix")
    c.FFN_G = c.g_seq


def _emit_attention(c):
    T = c.T
    T.emit("gpsimd", lambda e: e.dma_start(out=c.ea_f, in_=c.ea_d),
           waits=(("s_gmm", T.v("s_gmm")),), incs=(("s_ea", 16),))
    EA = 16

    c.PREATT_GATE = gev_now(c)
    grp = 0
    qu = 0
    pe_grp_after_head = []
    kv_cnt = 0
    q_cnt = 0
    first = True
    for hi in range(H):
        h = hi
        par = hi % 2
        qs = QS_H[h]
        if hi >= 2:
            gate = [("s_ape", pe_grp_after_head[hi - 2])]
        else:
            gate = list(c.YTK_GATE) + [("s_mix", c.MIX)]
        gate_k = gate + [("s_akk", 16 * kv_cnt)]
        gate_v = gate + [("s_akv", 16 * kv_cnt)]
        for ck in range(32):
            r0 = (ck % 4) * 4096 + h * 128
            cl = (ck // 4) * 128
            T.emit("sync", lambda e, par=par, ck=ck, r0=r0, cl=cl: e.dma_start(
                out=c.kun[par][:, 128 * ck:128 * (ck + 1)],
                in_=c.cck_o.ap()[r0:r0 + 128, cl:cl + 128]),
                waits=tuple(gate_k) + (("s_ccK", 1),), incs=(("s_akk", 16),))
            vr = (ck % 4) * (H * 1024) + h * 1024 + (ck // 4) * 128
            T.emit("scalar", lambda e, par=par, ck=ck, vr=vr: e.dma_start(
                out=c.vun[par][:, 128 * ck:128 * (ck + 1)],
                in_=c.ccv_o.ap()[vr:vr + 128, :]),
                waits=tuple(gate_v) + (("s_ccV", 1),), incs=(("s_akv", 16),))
            gate_k, gate_v = [], []
        kv_cnt += 32
        T.emit("sync", lambda e, par=par, h=h: e.dma_start(
            out=c.qh[par], in_=c.qdr.ap()[h * 128:h * 128 + 128, :]),
            waits=(("s_st0", 16 * c.TOT_ST[0]), ("s_st1", 16 * c.TOT_ST[1]),
                   ("s_aq", 16 * q_cnt)),
            incs=(("s_aq", 16),))
        q_cnt += 1

        for t in range(NTT):
            for qb in range(NQB_H[h]):
                lo = c_lo(h, t)
                chunks = list(range(lo, 4 * t + 4))
                eoff = EA_OFF[(h, t, qb)]
                per_bank = 4 if qs == 128 else 8
                o_bk = 4 + (qu % 2)
                l_r0 = 64 * (qu % 2)
                ngr = (len(chunks) + per_bank - 1) // per_bank
                gfirst = grp
                for gi in range(ngr):
                    gch = chunks[gi * per_bank:(gi + 1) * per_bank]
                    sbk = grp % 2
                    for pi, ck in enumerate(gch):
                        T.emit("tensor",
                               lambda e, ck=ck, pi=pi, sbk=sbk, par=par, t=t, qb=qb, qs=qs, h=h:
                               e.matmul(
                                   c.bank(sbk)[:, qs * pi:qs * (pi + 1)],
                                   c.kun[par][:, 128 * ck:128 * (ck + 1)],
                                   c.qh[par][:, 128 * t + qs * qb:128 * t + qs * (qb + 1)],
                                   start=True, stop=True, skip_group_check=True),
                               waits=(("s_akk", 16 * kv_cnt), ("s_aq", 16 * q_cnt),
                                      ("s_aex", max(0, grp - 1))) + c.PREATT_GATE,
                               incs=(("s_asm", 1),) if pi == len(gch) - 1 else ())
                    SM = T.v("s_asm")
                    w = qs * len(gch)
                    T.emit("scalar", lambda e, sbk=sbk, w=w, par2=grp % 2: e.activation(
                        c.ering[par2][:, 0:w], c.bank(sbk, w=w), AF.Exp),
                        waits=(("s_asm", SM), ("s_apt", max(0, grp - 1)),
                               ("s_mix", c.MIX)),
                        incs=(("s_aex", 1),))
                    AEX = T.v("s_aex")
                    for pi, ck in enumerate(gch):
                        rel = ck - 4 * t
                        mo = mask_off(rel, qs, qb)
                        msk = c.onesb.broadcast_to([128, qs]) if mo is None \
                            else c.masks[:, mo:mo + qs]
                        T.emit("vector",
                               lambda e, pi=pi, ck=ck, par2=grp % 2, qs=qs,
                               eo=eoff + (ck - lo), msk=msk:
                               e.scalar_tensor_tensor(
                                   c.pring[par2][:, qs * pi:qs * (pi + 1)],
                                   c.ering[par2][:, qs * pi:qs * (pi + 1)],
                                   c.ea_f[:, eo:eo + 1], msk,
                                   ALU.mult, ALU.mult),
                               waits=(("s_aex", AEX), ("s_ea", EA),
                                      ("s_mk", 16), ("s_ape", max(0, grp - 1))),
                               incs=(("s_apt", 1),) if pi == len(gch) - 1 else ())
                    PT = T.v("s_apt")
                    for pi, ck in enumerate(gch):
                        cidx = gi * per_bank + pi
                        is_first = (cidx == 0)
                        is_last = (cidx == len(chunks) - 1)
                        T.emit("tensor",
                               lambda e, ck=ck, pi=pi, par=par, par2=grp % 2, qs=qs,
                               o_bk=o_bk, st=is_first, sp=is_last:
                               e.matmul(
                                   c.bank(o_bk, w=qs),
                                   c.vun[par][:, 128 * ck:128 * (ck + 1)],
                                   c.pring[par2][:, qs * pi:qs * (pi + 1)],
                                   start=st, stop=sp, skip_group_check=True),
                               waits=(("s_apt", PT), ("s_akv", 16 * kv_cnt),
                                      ("s_aoe", max(0, qu - 1)),))
                        T.emit("tensor",
                               lambda e, ck=ck, pi=pi, par2=grp % 2, qs=qs,
                               l_r0=l_r0, st=is_first, sp=is_last:
                               e.matmul(
                                   c.bank(6, w=qs, r0=l_r0, rn=1),
                                   c.onesb,
                                   c.pring[par2][:, qs * pi:qs * (pi + 1)],
                                   start=st, stop=sp, skip_group_check=True),
                               waits=(("s_alr", max(0, qu - 1)),),
                               incs=(("s_ape", 1),) if pi == len(gch) - 1 else ())
                    grp += 1
                APE = T.v("s_ape")
                T.emit("vector", lambda e, l_r0=l_r0, qs=qs: e.reciprocal(
                    c.rlq[0:1, 0:qs], c.bank(6, w=qs, r0=l_r0, rn=1)),
                    waits=(("s_ape", APE),), incs=(("s_alr", 1),))
                ALR = T.v("s_alr")
                T.emit("tensor", lambda e, qs=qs: e.matmul(
                    c.bank(7, w=qs), c.onesf, c.rlq[0:1, 0:qs],
                    start=True, stop=True, skip_group_check=True),
                    waits=(("s_alr", ALR), ("s_arb", max(0, qu))),
                    incs=(("s_arm", 1),))
                ARM = T.v("s_arm")
                T.emit("vector", lambda e, qs=qs: e.tensor_copy(
                    c.rbsb[:, 0:qs], c.bank(7, w=qs)),
                    waits=(("s_arm", ARM),), incs=(("s_arb", 1),))
                ARB = T.v("s_arb")
                dst0 = h * 1024 + 128 * t + qs * qb
                T.emit("vector", lambda e, o_bk=o_bk, qs=qs, dst0=dst0: e.tensor_tensor(
                    c.bigA[:, dst0:dst0 + qs], c.bank(o_bk, w=qs), c.rbsb[:, 0:qs],
                    ALU.mult),
                    waits=(("s_arb", ARB),),
                    incs=(("s_aoe", 1),))
                qu += 1
        pe_grp_after_head.append(grp)
    c.ATT_GRP = grp
    c.ATT_QU = qu
    c.APE_ALL = T.v("s_ape")
    c.AOE_ALL = T.v("s_aoe")
    c.AEX_ALL = T.v("s_aex")


def _emit_wout(c):
    T = c.T
    for j in range(8):
        T.emit("sync", lambda e, j=j: e.dma_start(out=c.rowb, in_=c.bo_d[j:j + 1, :]),
               waits=(("s_bob", j), ("s_ape", c.APE_ALL),
                      ("s_aoe", c.AOE_ALL), ("s_alr", c.ATT_QU)),
               incs=(("s_row", 16),))
        RW = T.v("s_row")
        T.emit("tensor", lambda e: e.matmul(
            c.bank(7), c.onesf, c.rowb, start=True, stop=True,
            skip_group_check=True),
            waits=(("s_row", RW), ("s_bob", j), ("s_arb", c.ATT_QU)),
            incs=(("s_bom", 1),))
        BM = T.v("s_bom")
        T.emit("vector", lambda e, j=j: e.tensor_copy(
            c.bob[:, 512 * j:512 * (j + 1)], c.bank(7)),
            waits=(("s_bom", BM),), incs=(("s_bob", 1),))
    BOB = T.v("s_bob")

    def vstep(fn, extra=()):
        T.emit("vector", fn, waits=tuple(extra) + (("s_tv", T.v("s_tv")),),
               incs=(("s_tv", 1),))

    seq = 0
    c.out_lane = [0, 0]
    c.ax_lane = [0, 0]
    groups_after_wb = []
    for wb in range(32):
        par = wb % 2
        if wb < 2:
            gate = [("s_ape", c.APE_ALL)]
        else:
            gate = [("s_gmm", groups_after_wb[wb - 2])]
        wsem = f"s_w{par}"
        T.emit("gpsimd", lambda e, par=par, wb=wb: e.dma_start(
            out=c.band[par], in_=c.wout_d[wb]),
            waits=tuple(gate), incs=((wsem, 16),))
        WB = T.v(wsem)
        for tt in range(NTT):
            def mk_mm(e, i, bk, par=par, tt=tt):
                return e.matmul(
                    c.bank(bk, w=128),
                    c.bigA[:, 1024 * i + 128 * tt:1024 * i + 128 * (tt + 1)],
                    c.band[par][:, 128 * i:128 * (i + 1)],
                    start=(i == 0), stop=(i == 31), skip_group_check=True)
            def mk_ev(e, bk, wb=wb):
                return e.tensor_tensor(
                    c.ot1, c.bank(bk, w=128), c.bob[:, 128 * wb:128 * (wb + 1)],
                    ALU.add)
            _mm_group(c, 32, mk_mm, "vector", mk_ev,
                      extra_mm_waits=((wsem, WB), ("s_aoe", c.AOE_ALL),
                                      ("s_aex", c.AEX_ALL)),
                      extra_evac_waits=(("s_bob", BOB),))
            GEV1 = T.v("s_gevv")
            axs = f"s_ax{seq % 2}"
            T.emit("sync", lambda e, wb=wb, tt=tt, seq=seq: e.dma_start(
                out=c.xcol[seq % 2],
                in_=c.x_d[128 * tt:128 * (tt + 1), 128 * wb:128 * (wb + 1)]),
                waits=(("s_wev", max(0, seq - 1)),), incs=((axs, 16),))
            AXV = T.v(axs)
            T.emit("vector", lambda e, seq=seq: e.tensor_add(
                c.ot2, c.ot1, c.xcol[seq % 2]),
                waits=((axs, AXV), ("s_gevv", GEV1)),
                incs=(("s_wev", 1),))
            WEV1 = T.v("s_wev")
            T.emit("vector", lambda e, seq=seq, wb=wb, tt=tt: e.tensor_tensor(
                c.ostg[seq % 2], c.ot2,
                c.bigB[:, D * tt + 128 * wb:D * tt + 128 * (wb + 1)], ALU.add),
                waits=(("s_wev", WEV1),
                       (f"s_out{seq % 2}", 16 * c.out_lane[seq % 2]),
                       ("s_mix", c.MIX)),
                incs=(("s_w3", 1),))
            W3 = T.v("s_w3")
            T.emit("scalar", lambda e, seq=seq, wb=wb, tt=tt: e.dma_start(
                out=c.out_d[128 * tt:128 * (tt + 1), 128 * wb:128 * (wb + 1)],
                in_=c.ostg[seq % 2]),
                waits=(("s_w3", W3),), incs=((f"s_out{seq % 2}", 16),))
            c.out_lane[seq % 2] += 1
            seq += 1
        groups_after_wb.append(c.g_seq)


def _emit_debug(c):
    T = c.T
    if hasattr(c, "TOT_ST"):
        stw = (("s_st0", 16 * c.TOT_ST[0]), ("s_st1", 16 * c.TOT_ST[1]))
        T.emit("sync", lambda e: e.dma_start(out=c.dbg_q, in_=c.qdr.ap()),
               waits=stw, incs=(("s_outd", 16),))
        T.emit("sync", lambda e: e.dma_start(out=c.dbg_k, in_=c.cck_i.ap()),
               waits=stw, incs=(("s_outd", 16),))
        T.emit("sync", lambda e: e.dma_start(out=c.dbg_v, in_=c.ccv_i.ap()),
               waits=stw, incs=(("s_outd", 16),))
    T.emit("sync", lambda e: e.dma_start(out=c.dbg_bigB, in_=c.bigB),
           waits=(("s_xn", T.v("s_xn")), ("s_mix", T.v("s_mix"))),
           incs=(("s_outd", 16),))
    T.emit("sync", lambda e: e.dma_start(out=c.dbg_bigA, in_=c.bigA),
           waits=(("s_tev", T.v("s_tev")), ("s_aoe", T.v("s_aoe"))),
           incs=(("s_outd", 16),))
    T.emit("sync", lambda e: e.dma_start(out=c.dbg_f32[:, 0:192], in_=c.lgraw),
           waits=getattr(c, "LGR_GATE", ()), incs=(("s_outd", 16),))
    T.emit("sync", lambda e: e.dma_start(out=c.dbg_f32[:, 192:384], in_=c.lgs),
           incs=(("s_outd", 16),))
    T.emit("sync", lambda e: e.dma_start(out=c.dbg_f32[:, 384:512], in_=c.hres),
           waits=(("s_tv", T.v("s_tv")),), incs=(("s_outd", 16),))
    T.emit("sync", lambda e: e.dma_start(out=c.dbg_f32[:, 512:544], in_=c.hpre),
           incs=(("s_outd", 16),))
    T.emit("sync", lambda e: e.dma_start(out=c.dbg_f32[:, 544:576], in_=c.hpost),
           incs=(("s_outd", 16),))
    T.emit("sync", lambda e: e.dma_start(out=c.dbg_f32[:, 576:584], in_=c.rms_r),
           incs=(("s_outd", 16),))


def kernel(**inputs):
    shared, per_core = prep_inputs(inputs)
    nc = build_program()
    in_maps = [dict(shared, **pc) for pc in per_core]
    res = run_bass_kernel_spmd(nc, in_maps, core_ids=list(range(N_CORES)))
    out = np.zeros((B, S, D), dtype=np.float32)
    for cc in range(N_CORES):
        b, ci = cc // 4, cc % 4
        oc = res.results[cc]["out_core"]
        for t in range(NTT):
            out[b, 128 * (4 * t + ci):128 * (4 * t + ci) + 128, :] = \
                oc[128 * t:128 * (t + 1), :]
    return out


# version 07
# speedup vs baseline: 2.1416x; 2.1416x over previous
import math
import numpy as np
import ml_dtypes
import concourse.bass as bass
import concourse.mybir as mybir
from concourse.bass_utils import run_bass_kernel_spmd

F32 = mybir.dt.float32
BF16 = mybir.dt.bfloat16
U8 = mybir.dt.uint8
AF = mybir.ActivationFunctionType
ALU = mybir.AluOpType
AX = mybir.AxisListType
BFD = ml_dtypes.bfloat16

B, S, D, H, HD = 2, 4096, 4096, 32, 128
NT, NTT = 1024, 8
N_CORES = 8
EPS_LN = 1e-5
EPS_RMS = 1.1920929e-07
QSCALE = float(np.float32(1.0) / np.float32(math.sqrt(HD)))

_LIMIT = "all"
_DBG = False
_FFN_PART = 1000


def _slopes():
    m = np.arange(1, 33, dtype=np.float32) * np.float32(0.25)
    return (np.float32(1.0) / np.power(np.float32(2.0), m)).astype(np.float32)

SLOPES = _slopes()
ALIBI = SLOPES[:, None] * np.arange(1 - S, 1, dtype=np.float32)[None, :]
QS_H = [64 if h < 4 else 128 for h in range(H)]
NQB_H = [2 if h < 4 else 1 for h in range(H)]
WCH = [max(1, int(math.ceil(math.ceil(96.0 / float(SLOPES[h])) / 128.0)))
       for h in range(H)]

def c_lo(h, t):
    return max(0, 4 * t - WCH[h])

ATT_UNITS = []
EA_OFF = {}
_nea = 0
for _h in range(H):
    for _t in range(NTT):
        for _qb in range(NQB_H[_h]):
            ch = list(range(c_lo(_h, _t), 4 * _t + 4))
            ATT_UNITS.append((_h, _t, _qb, ch))
            EA_OFF[(_h, _t, _qb)] = _nea
            _nea += len(ch)
NEA = _nea
assert NEA <= 4096, NEA

def ea_table(ci):
    tab = np.zeros((128, NEA), dtype=np.float32)
    p = np.arange(128)
    for (h, t, qb, ch) in ATT_UNITS:
        qs = QS_H[h]
        q_last = 128 * (4 * t + ci) + (qb + 1) * qs - 1
        C = ALIBI[h, q_last]
        off = EA_OFF[(h, t, qb)]
        for i, c in enumerate(ch):
            a = ALIBI[h, 128 * c + p]
            tab[:, off + i] = np.exp(np.minimum(a - C, np.float32(0.0)))
    return tab

def mask_table(ci):
    tab = np.zeros((128, 1024), dtype=np.float32)
    tk = np.arange(128)[:, None]
    j128 = np.arange(128)[None, :]
    j64 = np.arange(64)[None, :]
    for rel in range(4):
        tab[:, rel * 128:(rel + 1) * 128] = (tk + 128 * (rel - ci) <= j128)
        for qb in range(2):
            base = 512 + (rel * 2 + qb) * 64
            tab[:, base:base + 64] = (tk + 128 * (rel - ci) - 64 * qb <= j64)
    return tab

def mask_off(rel, qs, qb):
    if rel < 0:
        return None
    return rel * 128 if qs == 128 else 512 + (rel * 2 + qb) * 64


def prep_inputs(inp):
    x = np.asarray(inp["L0_attn_x"], dtype=np.float32)
    w_qkv = np.asarray(inp["L0_attn_w_qkv"], dtype=np.float32)
    b_qkv = np.asarray(inp["L0_attn_b_qkv"], dtype=np.float32)
    w_out = np.asarray(inp["L0_attn_w_out"], dtype=np.float32)
    b_out = np.asarray(inp["L0_attn_b_out"], dtype=np.float32)
    W_res = np.asarray(inp["L0_ffn_W_res"], dtype=np.float32)
    W_pre = np.asarray(inp["L0_ffn_W_pre"], dtype=np.float32)
    W_post = np.asarray(inp["L0_ffn_W_post"], dtype=np.float32)
    H_res_b = np.asarray(inp["L0_ffn_H_res_base"], dtype=np.float32)
    H_pre_b = np.asarray(inp["L0_ffn_H_pre_base"], dtype=np.float32)
    H_post_b = np.asarray(inp["L0_ffn_H_post_base"], dtype=np.float32)
    gate = np.asarray(inp["L0_ffn_gate"], dtype=np.float32)
    W_branch = np.asarray(inp["L0_ffn_W_branch"], dtype=np.float32)

    wqkvT = np.ascontiguousarray(w_qkv.T)
    wqk = wqkvT[:, :8192].reshape(32, 128, 64, 128).transpose(2, 1, 0, 3)
    wqk_tiled = np.ascontiguousarray(wqk).reshape(64, 128, 4096)
    wv = wqkvT[:, 8192:].reshape(32, 128, 32, 128).transpose(2, 1, 0, 3)
    wv_tiled = np.ascontiguousarray(wv).reshape(32, 128, 4096)
    woT = np.ascontiguousarray(w_out.T)
    wo = woT.reshape(32, 128, 32, 128).transpose(2, 1, 0, 3)
    wout_tiled = np.ascontiguousarray(wo).reshape(32, 128, 4096)
    wbr = W_branch.reshape(8, 128, 8, 128).transpose(1, 2, 0, 3)
    wbr_tiled = np.ascontiguousarray(wbr).reshape(128, 8192)
    wsm = np.concatenate([W_res.T, W_pre.T, W_post.T], axis=1)
    wsm_tiled = np.ascontiguousarray(
        wsm.reshape(32, 128, 24).transpose(1, 0, 2)).reshape(128, 768)
    bias_pf = np.ascontiguousarray(b_qkv.reshape(96, 128).T)
    hsmall = np.concatenate([H_res_b.reshape(-1), H_pre_b, H_post_b, gate])
    shared = dict(
        wqk_tiled=wqk_tiled, wv_tiled=wv_tiled, wout_tiled=wout_tiled,
        wbr_tiled=wbr_tiled, wsm_tiled=wsm_tiled, bias_pf=bias_pf,
        b_v=np.ascontiguousarray(b_qkv[8192:]).reshape(8, 512),
        b_out=np.ascontiguousarray(b_out).reshape(8, 512),
        hsmall=hsmall.astype(np.float32).reshape(1, 25),
    )
    per_core = []
    for c in range(N_CORES):
        b, ci = c // 4, c % 4
        xs = np.concatenate(
            [x[b, 128 * (4 * t + ci):128 * (4 * t + ci) + 128, :]
             for t in range(NTT)], axis=0)
        per_core.append(dict(x_core=np.ascontiguousarray(xs),
                             masks01=mask_table(ci), ea_tab=ea_table(ci)))
    return shared, per_core


ENGS = ("sync", "gpsimd", "tensor", "vector", "scalar")

class Tape:
    def __init__(self, nc):
        self.nc = nc
        self.ops = {e: [] for e in ENGS}
        self.sems = {}
        self.cnt = {}

    def sem(self, name):
        if name not in self.sems:
            self.sems[name] = self.nc.alloc_semaphore(name)
            self.cnt[name] = 0
        return self.sems[name]

    def emit(self, eng, fn, waits=(), incs=()):
        rw = [(self.sem(s), int(val)) for s, val in waits if int(val) > 0]
        ri = []
        for s, n in incs:
            ri.append((self.sem(s), n))
            self.cnt[s] += n
        self.ops[eng].append((rw, fn, ri))

    def wait(self, eng, name, val):
        if int(val) > 0:
            self.ops[eng].append(([(self.sem(name), int(val))], None, []))

    def v(self, name):
        self.sem(name)
        return self.cnt[name]

    def replay(self, eng, engine):
        seen = {}
        for rw, fn, ri in self.ops[eng]:
            for s, val in rw:
                if seen.get(id(s), -1) >= val:
                    continue
                engine.wait_ge(s, val)
                seen[id(s)] = val
            if fn is None:
                continue
            ins = fn(engine)
            for s, n in ri:
                ins.then_inc(s, n)


class Ctx:
    pass


def build_program():
    nc = bass.Bass("TRN2", target_bir_lowering=False,
                   dynamic_dma_scratch_size=4096)
    c = Ctx()
    c.nc = nc

    c.x_d = nc.dram_tensor("x_core", [NT, D], F32, kind="ExternalInput").ap()
    c.wqk_d = nc.dram_tensor("wqk_tiled", [64, 128, 4096], F32, kind="ExternalInput").ap()
    c.wv_d = nc.dram_tensor("wv_tiled", [32, 128, 4096], F32, kind="ExternalInput").ap()
    c.wout_d = nc.dram_tensor("wout_tiled", [32, 128, 4096], F32, kind="ExternalInput").ap()
    c.wbr_d = nc.dram_tensor("wbr_tiled", [128, 8192], F32, kind="ExternalInput").ap()
    c.wsm_d = nc.dram_tensor("wsm_tiled", [128, 768], F32, kind="ExternalInput").ap()
    c.bias_d = nc.dram_tensor("bias_pf", [128, 96], F32, kind="ExternalInput").ap()
    c.bv_d = nc.dram_tensor("b_v", [8, 512], F32, kind="ExternalInput").ap()
    c.bo_d = nc.dram_tensor("b_out", [8, 512], F32, kind="ExternalInput").ap()
    c.hs_d = nc.dram_tensor("hsmall", [1, 25], F32, kind="ExternalInput").ap()
    c.mk_d = nc.dram_tensor("masks01", [128, 1024], F32, kind="ExternalInput").ap()
    c.ea_d = nc.dram_tensor("ea_tab", [128, NEA], F32, kind="ExternalInput").ap()
    c.out_d = nc.dram_tensor("out_core", [NT, D], F32, kind="ExternalOutput").ap()

    c.qdr = nc.dram_tensor("q_scr", [4096, 1024], BF16)
    c.cck_i = nc.dram_tensor("cc_in_k", [4096, 1024], BF16)
    c.ccv_i = nc.dram_tensor("cc_in_v", [H * 1024, 128], BF16)
    c.cck_o = nc.dram_tensor("cc_out_k", [4 * 4096, 1024], BF16)
    c.ccv_o = nc.dram_tensor("cc_out_v", [4 * H * 1024, 128], BF16)

    c.ident_d = nc.inline_tensor(np.eye(128, dtype=BFD), "ident_d")
    c.i24_d = nc.inline_tensor(np.eye(24, dtype=np.float32), "i24_d")
    c.onesf_d = nc.inline_tensor(np.ones((1, 128), dtype=np.float32), "onesf_d")
    c.onesb_d = nc.inline_tensor(np.ones((128, 1), dtype=BFD), "onesb_d")
    c.eps_d = nc.inline_tensor(
        np.concatenate([np.full((128, 1), EPS_LN, dtype=np.float32),
                        np.full((128, 1), EPS_RMS, dtype=np.float32)], axis=1),
        "eps_d")

    if _DBG:
        c.dbg_bigB = nc.dram_tensor("dbg_bigB", [128, 8 * D], BF16, kind="ExternalOutput").ap()
        c.dbg_bigA = nc.dram_tensor("dbg_bigA", [128, 32 * NT], BF16, kind="ExternalOutput").ap()
        c.dbg_f32 = nc.dram_tensor("dbg_f32", [128, 1024], F32, kind="ExternalOutput").ap()
        c.dbg_q = nc.dram_tensor("dbg_q", [4096, 1024], BF16, kind="ExternalOutput").ap()
        c.dbg_k = nc.dram_tensor("dbg_k", [4096, 1024], BF16, kind="ExternalOutput").ap()
        c.dbg_v = nc.dram_tensor("dbg_v", [H * 1024, 128], BF16, kind="ExternalOutput").ap()

    def arena(name, nbytes):
        return nc.alloc_sbuf_tensor(name, [128, nbytes], U8)[:]
    R2 = arena("R2", 16384)
    R3 = arena("R3", 16384)
    R4 = arena("R4", 16384)
    R5 = arena("R5", 8192)
    R7 = arena("R7", 8192)
    R8 = arena("R8", 4096)

    c.xt = R2[:, 0:16384].bitcast(F32)
    c.wbr_bf = R2[:, 0:16384].bitcast(BF16)
    c.ea_f = R2[:, 0:4 * NEA].bitcast(F32)
    c.band = [R3[:, 0:8192].bitcast(BF16), R4[:, 0:8192].bitcast(BF16)]
    c.xpreT = R3[:, 0:16384].bitcast(BF16)
    c.juggle = R4[:, 0:16384].bitcast(BF16)
    c.kun = [R3[:, 0:8192].bitcast(BF16), R4[:, 0:8192].bitcast(BF16)]
    c.vun = [R3[:, 8192:16384].bitcast(BF16), R4[:, 8192:16384].bitcast(BF16)]
    c.ytok = R3[:, 0:16384].bitcast(BF16)
    c.scrbf = R5[:, 0:8192].bitcast(BF16)
    c.ering = [R5[:, 0:1024].bitcast(BF16), R5[:, 1024:2048].bitcast(BF16)]
    c.pring = [R5[:, 2048:3072].bitcast(BF16), R5[:, 3072:4096].bitcast(BF16)]
    c.rbsb = R5[:, 4096:6144].bitcast(F32)
    c.rlq = R5[:, 6144:8192].bitcast(F32)
    c.lgf = R5[:, 0:4096].bitcast(F32)[0:24, :]
    c.rowb = R5[0:1, 4096:6144].bitcast(F32)
    c.stage = [R8[:, 0:1024].bitcast(BF16), R8[:, 1024:2048].bitcast(BF16)]
    c.xpre = R8[:, 0:4096].bitcast(F32)
    c.bvb = R7[:, 0:8192].bitcast(BF16)
    c.bob = R7[:, 0:8192].bitcast(BF16)

    al = nc.alloc_sbuf_tensor
    c.bigA = al("bigA", [128, 32 * NT], BF16)[:]
    c.bigB = al("bigB", [128, 8 * D], BF16)[:]
    c.masks = al("masks", [128, 1024], BF16)[:]
    c.onesb = al("onesb", [128, 1], BF16)[:]
    c.ident = al("ident", [128, 128], BF16)[:]
    c.i24 = al("i24", [24, 24], F32)[:]
    c.onesf = al("onesf", [1, 128], F32)[:]
    c.bias_sb = al("bias_sb", [128, 96], F32)[:]
    c.bias_q = al("bias_q", [128, 32], F32)[:]
    c.hs = al("hs", [1, 32], F32)[:]
    c.base24 = al("base24", [128, 24], F32)[:]
    c.wsm_bf = al("wsm_bf", [128, 768], BF16)[:]
    c.lgraw = al("lgraw", [128, 192], F32)[:]
    c.lgs = al("lgs", [128, 192], F32)[:]
    c.zbuf = al("zbuf", [128, 128], F32)[:]
    c.abuf = al("abuf", [128, 128], F32)[:]
    c.redu = al("redu", [128, 32], F32)[:]
    c.lnsu = al("lnsu", [128, 32], F32)[:]
    c.lnsv = al("lnsv", [128, 32], F32)[:]
    c.hres = al("hres", [128, 128], F32)[:]
    c.hpre = al("hpre", [128, 32], F32)[:]
    c.hpost = al("hpost", [128, 32], F32)[:]
    c.gcol = al("gcol", [128, 2], F32)[:]
    c.epsc = al("epsc", [128, 2], F32)[:]
    c.g1 = al("g1", [1, 3], F32)[:]
    c.stat = al("stat", [128, 16], F32)[:]
    c.rms_r = al("rms_r", [128, 8], F32)[:]
    c.qh = [al(f"qh{i}", [128, 1024], BF16)[:] for i in range(2)]
    c.ot1 = al("ot1", [128, 128], F32)[:]
    c.ot2 = al("ot2", [128, 128], F32)[:]
    c.xcol = [al(f"xcol{i}", [128, 128], F32)[:] for i in range(2)]
    c.ostg = [al(f"ostg{i}", [128, 128], F32)[:] for i in range(2)]

    c.ps = nc.alloc_psum_tensor("pspool", [128, 4096], F32)[:]
    c.bank = lambda i, w=512, r0=0, rn=128: c.ps[r0:r0 + rn, 512 * i:512 * i + w]

    T = Tape(nc)
    c.T = T

    c.g_seq = 0
    c.evac_info = []

    _emit_preload(c)
    _emit_p0(c)
    _emit_bvb(c)
    if _LIMIT != "p0":
        _emit_p1(c)
    if _LIMIT not in ("p0", "p1"):
        _emit_ffn(c)
    if _LIMIT not in ("p0", "p1", "ffn"):
        _emit_attention(c)
    if _LIMIT == "all":
        _emit_wout(c)
    if _DBG:
        _emit_debug(c)

    T.wait("sync", "s_out0", T.v("s_out0"))
    T.wait("sync", "s_out1", T.v("s_out1"))
    T.wait("sync", "s_outd", T.v("s_outd"))

    with nc.Block() as block:
        @block.sync
        def _(sy):
            T.replay("sync", sy)

        @block.gpsimd
        def _(g):
            T.replay("gpsimd", g)

        @block.tensor
        def _(t):
            T.replay("tensor", t)

        @block.vector
        def _(v):
            T.replay("vector", v)

        @block.scalar
        def _(s):
            T.replay("scalar", s)
    return nc


def _emit_preload(c):
    T = c.T
    for dst, srcap in [(c.onesb, c.onesb_d.ap()), (c.epsc, c.eps_d.ap()),
                       (c.ident, c.ident_d.ap()), (c.i24, c.i24_d.ap()),
                       (c.onesf, c.onesf_d.ap()), (c.bias_sb, c.bias_d),
                       (c.hs[0:1, 0:25], c.hs_d)]:
        T.emit("sync", lambda e, d=dst, s=srcap: e.dma_start(out=d, in_=s),
               incs=(("s_pre", 16),))
    T.emit("gpsimd", lambda e: e.dma_start(out=c.masks, in_=c.mk_d),
           incs=(("s_mk", 16),))
    c.PRE = T.v("s_pre")
    T.emit("vector", lambda e: e.tensor_scalar_mul(c.bias_q, c.bias_sb[:, 0:32], QSCALE),
           waits=(("s_pre", c.PRE),), incs=(("s_bq", 1),))


def _emit_bvb(c):
    T = c.T
    for j in range(8):
        T.emit("sync", lambda e, j=j: e.dma_start(out=c.rowb, in_=c.bv_d[j:j + 1, :]),
               waits=(("s_bvb", j),), incs=(("s_row", 16),))
        T.emit("tensor", lambda e, j=j: e.matmul(
            c.bank(6 + j % 2), c.onesf, c.rowb, start=True, stop=True,
            skip_group_check=True),
            waits=(("s_row", 16 * (j + 1)), ("s_bvb", j)), incs=(("s_bvm", 1),))
        T.emit("vector", lambda e, j=j: e.tensor_copy(
            c.bvb[:, 512 * j:512 * (j + 1)], c.bank(6 + j % 2)),
            waits=(("s_bvm", j + 1),), incs=(("s_bvb", 1),))
    c.BVB = T.v("s_bvb")


def _emit_p0(c):
    T = c.T
    ps = c.ps
    tgrp = 0
    for tc in range(NTT):
        w = [("s_xn", tc)] if tc > 0 else []
        T.emit("sync", lambda e, tc=tc: e.dma_start(
            out=c.xt, in_=c.x_d[128 * tc:128 * (tc + 1), :]),
            waits=tuple(w), incs=(("s_x", 16),))
        XV = T.v("s_x")
        T.emit("vector", lambda e: e.reduce_sum(c.stat[:, 0:1], c.xt, AX.X),
               waits=(("s_x", XV),), incs=(("s_sum", 1),))
        T.emit("scalar", lambda e: e.activation(
            ps[:, 1024:3072], c.xt[:, 0:2048], AF.Square, accum_out=c.stat[:, 1:2]),
            waits=(("s_x", XV),), incs=(("s_ta", 1),))
        T.emit("scalar", lambda e: e.activation(
            ps[:, 1024:3072], c.xt[:, 2048:4096], AF.Square, accum_out=c.stat[:, 2:3]),
            waits=(("s_ta", T.v("s_ta")),), incs=(("s_sq", 1),))
        SQ, SM = T.v("s_sq"), T.v("s_sum")

        def vstep(fn, extra=()):
            T.emit("vector", fn, waits=tuple(extra) + (("s_tv", T.v("s_tv")),),
                   incs=(("s_tv", 1),))

        def astep(fn, extra=()):
            T.emit("scalar", fn, waits=tuple(extra) + (("s_ta", T.v("s_ta")),),
                   incs=(("s_ta", 1),))

        vstep(lambda e: e.tensor_add(c.stat[:, 3:4], c.stat[:, 1:2], c.stat[:, 2:3]),
              ((("s_sq"), SQ),))
        vstep(lambda e: e.tensor_scalar_mul(c.stat[:, 4:5], c.stat[:, 0:1], 1.0 / D),
              ((("s_sum"), SM),))
        vstep(lambda e: e.tensor_scalar_mul(c.stat[:, 5:6], c.stat[:, 3:4], 1.0 / D))
        vstep(lambda e: e.tensor_mul(c.stat[:, 6:7], c.stat[:, 4:5], c.stat[:, 4:5]))
        vstep(lambda e: e.tensor_sub(c.stat[:, 7:8], c.stat[:, 5:6], c.stat[:, 6:7]))
        TV = T.v("s_tv")
        astep(lambda e: e.activation(c.stat[:, 8:9], c.stat[:, 7:8], AF.Sqrt,
                                     bias=c.epsc[:, 0:1]), ((("s_tv"), TV),))
        TA = T.v("s_ta")
        vstep(lambda e: e.reciprocal(c.stat[:, 9:10], c.stat[:, 8:9]),
              ((("s_ta"), TA),))
        vstep(lambda e: e.tensor_mul(c.stat[:, 11:12], c.stat[:, 7:8], c.stat[:, 9:10]))
        vstep(lambda e: e.tensor_mul(c.stat[:, 12:13], c.stat[:, 11:12], c.stat[:, 9:10]))
        TV = T.v("s_tv")
        astep(lambda e: e.activation(c.stat[:, 13:14], c.stat[:, 12:13], AF.Sqrt,
                                     bias=c.epsc[:, 1:2]), ((("s_tv"), TV),))
        TA = T.v("s_ta")
        vstep(lambda e, tc=tc: e.reciprocal(c.rms_r[:, tc:tc + 1], c.stat[:, 13:14]),
              ((("s_ta"), TA),))
        TV = T.v("s_tv")
        T.emit("vector", lambda e, tc=tc: e.tensor_scalar(
            c.bigB[:, D * tc:D * (tc + 1)], c.xt, c.stat[:, 4:5], c.stat[:, 9:10],
            ALU.subtract, ALU.mult),
            waits=(("s_tv", TV),), incs=(("s_xn", 1),))
        XN = T.v("s_xn")
        for g in range(8):
            bk = tgrp % 2
            for q in range(4):
                dc = 4 * g + q
                incs = (("s_tmm", 1),) if q == 3 else ()
                T.emit("tensor", lambda e, tc=tc, dc=dc, bk=bk, q=q: e.matmul(
                    c.bank(bk)[:, 128 * q:128 * (q + 1)],
                    c.bigB[:, D * tc + 128 * dc:D * tc + 128 * (dc + 1)], c.ident,
                    start=True, stop=True, skip_group_check=True),
                    waits=(("s_xn", XN), ("s_pre", c.PRE),
                           ("s_tev", max(0, tgrp - 1))),
                    incs=incs)
            tgrp += 1
            TM = T.v("s_tmm")
            T.emit("scalar", lambda e, tc=tc, g=g, bk=bk: e.activation(
                c.bigA.rearrange("p (c t) -> p c t", t=NT)
                    [:, 4 * g:4 * g + 4, 128 * tc:128 * (tc + 1)],
                c.bank(bk).rearrange("p (c t) -> p c t", t=128),
                AF.Identity),
                waits=(("s_tmm", TM),), incs=(("s_tev", 1),))
    c.P0_XN = T.v("s_xn")
    c.P0_TEV = T.v("s_tev")


def gev_now(c):
    w = []
    for s in ("s_geva", "s_gevv"):
        v = c.T.v(s)
        if v:
            w.append((s, v))
    return tuple(w)


def _mm_group(c, n_mm, mk_mm, evac_eng, mk_evac, extra_mm_waits=(),
              extra_evac_waits=()):
    T = c.T
    g = c.g_seq
    bk = g % 4
    c.g_seq += 1
    war = (c.evac_info[g - 4],) if g >= 4 else ()
    for i in range(n_mm):
        incs = (("s_gmm", 1),) if i == n_mm - 1 else ()
        T.emit("tensor", lambda e, i=i, bk=bk: mk_mm(e, i, bk),
               waits=tuple(extra_mm_waits) + war,
               incs=incs)
    GM = T.v("s_gmm")
    esem = "s_geva" if evac_eng == "scalar" else "s_gevv"
    T.emit(evac_eng, lambda e, bk=bk: mk_evac(e, bk),
           waits=tuple(extra_evac_waits) + (("s_gmm", GM),),
           incs=((esem, 1),))
    c.evac_info.append((esem, T.v(esem)))
    return g


def _emit_p1(c):
    T = c.T
    band_plan = ([("K", i) for i in range(32, 64)] +
                 [("V", i) for i in range(32)] +
                 [("Q", i) for i in range(32)])
    groups_after_band = []
    nst = 0
    c.st_lane = [0, 0]
    c.k_store = None
    c.v_store = None
    for bi, (kind, idx) in enumerate(band_plan):
        par = bi % 2
        src = c.wqk_d if kind in ("K", "Q") else c.wv_d
        si = idx if kind != "V" else idx
        w = [] if bi < 2 else [("s_gmm", groups_after_band[bi - 2])]
        wsem = f"s_w{par}"
        T.emit("gpsimd", lambda e, par=par, src=src, si=si: e.dma_start(
            out=c.band[par], in_=src[si]),
            waits=tuple(w), incs=((wsem, 16),))
        WB = T.v(wsem)
        if kind in ("K", "Q"):
            for tb in range(2):
                def mk_mm(e, i, bk, par=par, tb=tb, WB=WB):
                    return e.matmul(
                        c.bank(bk), c.band[par][:, 128 * i:128 * (i + 1)],
                        c.bigA[:, 1024 * i + 512 * tb:1024 * i + 512 * (tb + 1)],
                        start=(i == 0), stop=(i == 31), skip_group_check=True)
                stg = c.stage[nst % 2]
                if kind == "Q":
                    def mk_ev(e, bk, stg=stg, idx=idx):
                        return e.activation(stg, c.bank(bk), AF.Identity,
                                            bias=c.bias_q[:, idx:idx + 1],
                                            scale=QSCALE)
                    ew = [("s_bq", 1)]
                else:
                    def mk_ev(e, bk, stg=stg, idx=idx):
                        return e.activation(stg, c.bank(bk), AF.Identity,
                                            bias=c.bias_sb[:, idx:idx + 1])
                    ew = []
                ew.append((f"s_st{nst % 2}", 16 * c.st_lane[nst % 2]))
                _mm_group(c, 32, mk_mm, "scalar", mk_ev,
                          extra_mm_waits=((wsem, WB), ("s_tev", c.P0_TEV)),
                          extra_evac_waits=tuple(ew))
                EV = c.evac_info[-1]
                dst = c.cck_i if kind == "K" else c.qdr
                r0 = (idx - 32) * 128 if kind == "K" else idx * 128
                T.emit("sync", lambda e, dst=dst, r0=r0, tb=tb, stg=stg: e.dma_start(
                    out=dst.ap()[r0:r0 + 128, 512 * tb:512 * (tb + 1)], in_=stg),
                    waits=(EV,), incs=((f"s_st{nst % 2}", 16),))
                c.st_lane[nst % 2] += 1
                nst += 1
        else:
            for tc in range(NTT):
                def mk_mm(e, i, bk, par=par, tc=tc):
                    return e.matmul(
                        c.bank(bk, w=128),
                        c.bigA[:, 1024 * i + 128 * tc:1024 * i + 128 * (tc + 1)],
                        c.band[par][:, 128 * i:128 * (i + 1)],
                        start=(i == 0), stop=(i == 31), skip_group_check=True)
                stg = c.stage[nst % 2]
                def mk_ev(e, bk, stg=stg, idx=idx):
                    return e.tensor_tensor(stg[:, 0:128], c.bank(bk, w=128),
                                           c.bvb[:, 128 * idx:128 * (idx + 1)],
                                           ALU.add)
                _mm_group(c, 32, mk_mm, "vector", mk_ev,
                          extra_mm_waits=((wsem, WB), ("s_tev", c.P0_TEV)),
                          extra_evac_waits=(("s_bvb", c.BVB),
                                            (f"s_st{nst % 2}", 16 * c.st_lane[nst % 2])))
                EV = c.evac_info[-1]
                r0 = idx * 1024 + tc * 128
                T.emit("sync", lambda e, r0=r0, stg=stg: e.dma_start(
                    out=c.ccv_i.ap()[r0:r0 + 128, :], in_=stg[:, 0:128]),
                    waits=(EV,), incs=((f"s_st{nst % 2}", 16),))
                c.st_lane[nst % 2] += 1
                nst += 1
        groups_after_band.append(c.g_seq)
        if kind == "K" and idx == 63:
            c.k_store = tuple(c.st_lane)
        if kind == "V" and idx == 31:
            c.v_store = tuple(c.st_lane)

    T.emit("gpsimd", lambda e: e.collective_compute(
        "AllGather", ALU.bypass, replica_groups=[[0, 1, 2, 3], [4, 5, 6, 7]],
        ins=[c.cck_i.ap().opt()], outs=[c.cck_o.ap().opt()]),
        waits=(("s_st0", 16 * c.k_store[0]), ("s_st1", 16 * c.k_store[1])),
        incs=(("s_ccK", 1),))
    T.emit("gpsimd", lambda e: e.collective_compute(
        "AllGather", ALU.bypass, replica_groups=[[0, 1, 2, 3], [4, 5, 6, 7]],
        ins=[c.ccv_i.ap().opt()], outs=[c.ccv_o.ap().opt()]),
        waits=(("s_st0", 16 * c.v_store[0]), ("s_st1", 16 * c.v_store[1])),
        incs=(("s_ccV", 1),))

    T.emit("gpsimd", lambda e: e.dma_start(out=c.wsm_bf, in_=c.wsm_d),
           waits=(("s_gmm", groups_after_band[-2]),), incs=(("s_wsm", 16),))
    WSM = 16
    for tb in range(2):
        def mk_mm(e, i, bk, tb=tb):
            return e.matmul(
                c.bank(bk, r0=0, rn=24), c.wsm_bf[:, 24 * i:24 * (i + 1)],
                c.bigA[:, 1024 * i + 512 * tb:1024 * i + 512 * (tb + 1)],
                start=(i == 0), stop=(i == 31), skip_group_check=True)
        def mk_ev(e, bk, tb=tb):
            return e.activation(c.lgf[0:24, 512 * tb:512 * (tb + 1)],
                                c.bank(bk, r0=0, rn=24), AF.Identity)
        _mm_group(c, 32, mk_mm, "scalar", mk_ev,
                  extra_mm_waits=(("s_wsm", WSM),))
    lg_gate = gev_now(c)
    def mk_mm(e, i, bk):
        return e.matmul(
            c.bank(bk)[:, 24 * i:24 * (i + 1)],
            c.lgf[0:24, 128 * i:128 * (i + 1)], c.i24,
            start=True, stop=True, skip_group_check=True)
    def mk_ev(e, bk):
        return e.tensor_copy(c.lgraw, c.bank(bk, w=192))
    _mm_group(c, 8, mk_mm, "vector", mk_ev,
              extra_mm_waits=(("s_pre", c.PRE),) + lg_gate)
    c.LGR_GATE = gev_now(c)
    c.P1_GROUPS = c.g_seq
    c.TOT_ST = tuple(c.st_lane)
    c.BANDS_GMM = groups_after_band[-1]


def _emit_ffn(c):
    T = c.T

    def vstep(fn, extra=()):
        T.emit("vector", fn, waits=tuple(extra) + (("s_tv", T.v("s_tv")),),
               incs=(("s_tv", 1),))

    def astep(fn, extra=()):
        T.emit("scalar", fn, waits=tuple(extra) + (("s_ta", T.v("s_ta")),),
               incs=(("s_ta", 1),))

    _mm_group(c, 1,
              lambda e, i, bk: e.matmul(c.bank(bk, w=24), c.onesf,
                                        c.hs[0:1, 0:24], start=True, stop=True,
                                        skip_group_check=True),
              "vector",
              lambda e, bk: e.tensor_copy(c.base24, c.bank(bk, w=24)),
              extra_mm_waits=(("s_pre", c.PRE),))
    B24_GATE = gev_now(c)
    astep(lambda e: e.activation(c.g1[0:1, 0:1], c.hs[0:1, 24:25], AF.Exp,
                                 scale=-1.0), ((("s_pre"), c.PRE),))
    TA = T.v("s_ta")
    vstep(lambda e: e.tensor_scalar_add(c.g1[0:1, 1:2], c.g1[0:1, 0:1], 1.0),
          ((("s_ta"), TA),))
    vstep(lambda e: e.reciprocal(c.g1[0:1, 0:1], c.g1[0:1, 1:2]))
    vstep(lambda e: e.memset(c.g1[0:1, 2:3], 1.0))
    vstep(lambda e: e.tensor_sub(c.g1[0:1, 1:2], c.g1[0:1, 2:3], c.g1[0:1, 0:1]))
    TV = T.v("s_tv")
    _mm_group(c, 1,
              lambda e, i, bk: e.matmul(c.bank(bk, w=2), c.onesf,
                                        c.g1[0:1, 0:2], start=True, stop=True,
                                        skip_group_check=True),
              "vector",
              lambda e, bk: e.tensor_copy(c.gcol, c.bank(bk, w=2)),
              extra_mm_waits=(("s_tv", TV),))
    GC_GATE = gev_now(c)

    for tc in range(NTT):
        vstep(lambda e, tc=tc: e.scalar_tensor_tensor(
            c.lgs[:, 24 * tc:24 * (tc + 1)], c.lgraw[:, 24 * tc:24 * (tc + 1)],
            c.rms_r[:, tc:tc + 1], c.base24, ALU.mult, ALU.add),
            B24_GATE)

    lgs3 = c.lgs.rearrange("p (a c) -> p a c", c=24)
    for (dst, off) in ((c.hpre, 16), (c.hpost, 20)):
        d3 = dst.rearrange("p (a n) -> p a n", n=4)
        vstep(lambda e, off=off: e.reduce_max(c.redu[:, 0:8], lgs3[:, :, off:off + 4], AX.X))
        vstep(lambda e, off=off, d3=d3: e.tensor_tensor(
            d3, lgs3[:, :, off:off + 4],
            c.redu[:, 0:8].rearrange("p (a o) -> p a o", o=1).broadcast_to([128, 8, 4]),
            ALU.subtract))
        TV = T.v("s_tv")
        astep(lambda e, dst=dst: e.activation(dst, dst, AF.Exp), ((("s_tv"), TV),))
        TA = T.v("s_ta")
        vstep(lambda e, d3=d3: e.reduce_sum(c.redu[:, 8:16], d3, AX.X),
              ((("s_ta"), TA),))
        vstep(lambda e: e.reciprocal(c.redu[:, 16:24], c.redu[:, 8:16]))
        vstep(lambda e, d3=d3: e.tensor_tensor(
            d3, d3,
            c.redu[:, 16:24].rearrange("p (a o) -> p a o", o=1).broadcast_to([128, 8, 4]),
            ALU.mult))
    HPP = T.v("s_tv")
    if _FFN_PART < 2:
        c.MIX = 0
        c.YTK_GATE = ()
        c.FFN_G = c.g_seq
        return

    z3 = c.zbuf.rearrange("p (a i j) -> p a i j", i=4, j=4)
    z3j = c.zbuf.rearrange("p (a i j) -> p a j i", i=4, j=4)
    a3 = c.abuf.rearrange("p (a i j) -> p a i j", i=4, j=4)
    a3j = c.abuf.rearrange("p (a i j) -> p a j i", i=4, j=4)
    z2 = c.zbuf.rearrange("p (a e) -> p a e", e=16)
    r_i = lambda buf: buf.rearrange("p (a i) -> p a i", i=4).unsqueeze(3).broadcast_to([128, 8, 4, 4])
    r_j = lambda buf: buf.rearrange("p (a j) -> p a j", j=4).unsqueeze(2).broadcast_to([128, 8, 4, 4])

    vstep(lambda e: e.tensor_scalar_mul(z2, lgs3[:, :, 0:16], 10.0))
    vstep(lambda e: e.tensor_reduce(c.redu[:, 0:32], z3, AX.X, ALU.max))
    vstep(lambda e: e.tensor_tensor(z3, z3, r_i(c.redu[:, 0:32]), ALU.subtract))
    vstep(lambda e: e.tensor_reduce(c.redu[:, 0:32], z3j, AX.X, ALU.max))
    vstep(lambda e: e.tensor_tensor(z3j, z3j, r_i(c.redu[:, 0:32]), ALU.subtract))
    TA_prev = None
    for it in range(50):
        if it == 0:
            vstep(lambda e: e.tensor_scalar_max(c.abuf, c.zbuf, -80.0))
        else:
            vstep(lambda e: e.tensor_tensor(a3, z3, r_j(c.lnsv), ALU.subtract),
                  ((("s_ta"), TA_prev),))
            vstep(lambda e: e.tensor_scalar_max(c.abuf, c.abuf, -80.0))
        TV = T.v("s_tv")
        astep(lambda e: e.activation(c.abuf, c.abuf, AF.Exp), ((("s_tv"), TV),))
        TA = T.v("s_ta")
        vstep(lambda e: e.tensor_reduce(c.redu[:, 0:32], a3, AX.X, ALU.add),
              ((("s_ta"), TA),))
        TV = T.v("s_tv")
        astep(lambda e: e.activation(c.lnsu, c.redu[:, 0:32], AF.Ln), ((("s_tv"), TV),))
        TA = T.v("s_ta")
        vstep(lambda e: e.tensor_tensor(a3, z3, r_i(c.lnsu), ALU.subtract),
              ((("s_ta"), TA),))
        vstep(lambda e: e.tensor_scalar_max(c.abuf, c.abuf, -80.0))
        TV = T.v("s_tv")
        astep(lambda e: e.activation(c.abuf, c.abuf, AF.Exp), ((("s_tv"), TV),))
        TA = T.v("s_ta")
        vstep(lambda e: e.tensor_reduce(c.redu[:, 0:32], a3j, AX.X, ALU.add),
              ((("s_ta"), TA),))
        TV = T.v("s_tv")
        astep(lambda e: e.activation(c.lnsv, c.redu[:, 0:32], AF.Ln), ((("s_tv"), TV),))
        TA_prev = T.v("s_ta")
    TA = T.v("s_ta")
    vstep(lambda e: e.tensor_tensor(a3, z3, r_i(c.lnsu), ALU.subtract),
          ((("s_ta"), TA),))
    vstep(lambda e: e.tensor_tensor(a3, a3, r_j(c.lnsv), ALU.subtract))
    TV = T.v("s_tv")
    astep(lambda e: e.activation(c.hres, c.abuf, AF.Exp), ((("s_tv"), TV),))
    TA = T.v("s_ta")
    vstep(lambda e: e.tensor_scalar_mul(c.hres, c.hres, c.gcol[:, 0:1]),
          ((("s_ta"), TA),) + GC_GATE)
    h16 = c.hres.rearrange("p (a e) -> p a e", e=16)
    for i in range(4):
        vstep(lambda e, i=i: e.tensor_scalar_add(
            h16[:, :, 5 * i:5 * i + 1], h16[:, :, 5 * i:5 * i + 1], c.gcol[:, 1:2]))
    HRES = T.v("s_tv")
    if _FFN_PART < 25:
        c.MIX = 0
        c.YTK_GATE = ()
        c.FFN_G = c.g_seq
        return

    for tc in range(NTT):
        xb = lambda n, tc=tc: c.bigB[:, D * tc + 1024 * n:D * tc + 1024 * (n + 1)]
        vstep(lambda e, tc=tc, xb=xb: e.tensor_scalar_mul(
            c.xpre, xb(0), c.hpre[:, 4 * tc:4 * tc + 1]),
            ((("s_tv"), HPP), (("s_st0"), 16 * c.TOT_ST[0]),
             (("s_st1"), 16 * c.TOT_ST[1]), (("s_gmm"), c.BANDS_GMM)))
        for n in (1, 2):
            vstep(lambda e, tc=tc, n=n, xb=xb: e.scalar_tensor_tensor(
                c.xpre, xb(n), c.hpre[:, 4 * tc + n:4 * tc + n + 1], c.xpre,
                ALU.mult, ALU.add))
        vstep(lambda e, tc=tc, xb=xb: e.scalar_tensor_tensor(
            c.juggle[:, 1024 * tc:1024 * (tc + 1)], xb(3),
            c.hpre[:, 4 * tc + 3:4 * tc + 4], c.xpre, ALU.mult, ALU.add))
    XPRE = T.v("s_tv")

    def transpose_sweep(src, dst, gate):
        for tc in range(NTT):
            for gq in range(2):
                def mk_mm(e, q, bk, tc=tc, gq=gq, src=src):
                    cc = 4 * gq + q
                    return e.matmul(
                        c.bank(bk)[:, 128 * q:128 * (q + 1)],
                        src[:, 1024 * tc + 128 * cc:1024 * tc + 128 * (cc + 1)],
                        c.ident, start=True, stop=True, skip_group_check=True)
                def mk_ev(e, bk, tc=tc, gq=gq, dst=dst):
                    return e.activation(
                        dst.rearrange("p (c t) -> p c t", t=1024)
                           [:, 4 * gq:4 * gq + 4, 128 * tc:128 * (tc + 1)],
                        c.bank(bk).rearrange("p (c t) -> p c t", t=128),
                        AF.Identity)
                _mm_group(c, 4, mk_mm, "scalar", mk_ev, extra_mm_waits=gate)
        return gev_now(c)

    if _FFN_PART < 26:
        c.MIX = 0
        c.YTK_GATE = ()
        c.FFN_G = c.g_seq
        return
    XPT_GATE = transpose_sweep(c.juggle, c.xpreT, (("s_tv", XPRE),))
    if _FFN_PART < 255.5:
        c.MIX = 0
        c.YTK_GATE = ()
        c.FFN_G = c.g_seq
        return

    T.emit("gpsimd", lambda e: e.dma_start(out=c.wbr_bf, in_=c.wbr_d),
        waits=(("s_xn", c.P0_XN),), incs=(("s_wbr", 16),))
    WBR = 16
    if _FFN_PART < 256.5:
        c.MIX = 0
        c.YTK_GATE = ()
        c.FFN_G = c.g_seq
        return
    for occ in range(8):
        for tb in range(2):
            def mk_mm(e, i, bk, occ=occ, tb=tb):
                return e.matmul(
                    c.bank(bk), c.wbr_bf[:, 1024 * occ + 128 * i:1024 * occ + 128 * (i + 1)],
                    c.xpreT[:, 1024 * i + 512 * tb:1024 * i + 512 * (tb + 1)],
                    start=(i == 0), stop=(i == 7), skip_group_check=True)
            def mk_ev(e, bk, occ=occ, tb=tb):
                return e.activation(
                    c.juggle[:, 1024 * occ + 512 * tb:1024 * occ + 512 * (tb + 1)],
                    c.bank(bk), AF.Identity)
            _mm_group(c, 8, mk_mm, "scalar", mk_ev,
                      extra_mm_waits=(("s_wbr", WBR),) + XPT_GATE)
    YT_GATE = gev_now(c)

    if _FFN_PART < 27:
        c.MIX = 0
        c.YTK_GATE = ()
        c.FFN_G = c.g_seq
        return
    YTK_GATE = transpose_sweep(c.juggle, c.ytok, YT_GATE)
    c.YTK_GATE = YTK_GATE

    if _FFN_PART < 28:
        c.MIX = 0
        c.FFN_G = c.g_seq
        return

    for tc in range(NTT):
        xb = lambda n, tc=tc: c.bigB[:, D * tc + 1024 * n:D * tc + 1024 * (n + 1)]
        hcol = lambda n, j, tc=tc: c.hres[:, 16 * tc + 4 * n + j:16 * tc + 4 * n + j + 1]
        for n in range(4):
            vstep(lambda e, tc=tc, n=n, xb=xb, hcol=hcol: e.tensor_scalar_mul(
                c.xpre, xb(0), hcol(n, 0)), ((("s_tv"), HRES),) + YTK_GATE)
            for j in (1, 2, 3):
                vstep(lambda e, tc=tc, n=n, j=j, xb=xb, hcol=hcol: e.scalar_tensor_tensor(
                    c.xpre, xb(j), hcol(n, j), c.xpre, ALU.mult, ALU.add))
            vstep(lambda e, tc=tc, n=n, hcol=hcol: e.scalar_tensor_tensor(
                c.scrbf[:, 1024 * n:1024 * (n + 1)],
                c.ytok[:, 1024 * tc:1024 * (tc + 1)],
                c.hpost[:, 4 * tc + n:4 * tc + n + 1], c.xpre, ALU.mult, ALU.add))
        T.emit("vector", lambda e, tc=tc: e.tensor_copy(
            c.bigB[:, D * tc:D * (tc + 1)], c.scrbf),
            waits=(("s_tv", T.v("s_tv")),), incs=(("s_mix", 1),))
    c.MIX = T.v("s_mix")
    c.FFN_G = c.g_seq


def _emit_attention(c):
    T = c.T
    T.emit("gpsimd", lambda e: e.dma_start(out=c.ea_f, in_=c.ea_d),
           waits=(("s_gmm", T.v("s_gmm")),), incs=(("s_ea", 16),))
    EA = 16

    c.PREATT_GATE = gev_now(c)
    grp = 0
    qu = 0
    pe_grp_after_head = []
    kv_cnt = 0
    q_cnt = 0
    first = True
    for hi in range(H):
        h = hi
        par = hi % 2
        qs = QS_H[h]
        if hi >= 2:
            gate = [("s_ape", pe_grp_after_head[hi - 2])]
        else:
            gate = list(c.YTK_GATE) + [("s_mix", c.MIX)]
        gate_k = gate + [("s_akk", 16 * kv_cnt)]
        gate_v = gate + [("s_akv", 16 * kv_cnt)]
        for ck in range(32):
            r0 = (ck % 4) * 4096 + h * 128
            cl = (ck // 4) * 128
            T.emit("sync", lambda e, par=par, ck=ck, r0=r0, cl=cl: e.dma_start(
                out=c.kun[par][:, 128 * ck:128 * (ck + 1)],
                in_=c.cck_o.ap()[r0:r0 + 128, cl:cl + 128]),
                waits=tuple(gate_k) + (("s_ccK", 1),), incs=(("s_akk", 16),))
            vr = (ck % 4) * (H * 1024) + h * 1024 + (ck // 4) * 128
            T.emit("scalar", lambda e, par=par, ck=ck, vr=vr: e.dma_start(
                out=c.vun[par][:, 128 * ck:128 * (ck + 1)],
                in_=c.ccv_o.ap()[vr:vr + 128, :]),
                waits=tuple(gate_v) + (("s_ccV", 1),), incs=(("s_akv", 16),))
            gate_k, gate_v = [], []
        kv_cnt += 32
        T.emit("sync", lambda e, par=par, h=h: e.dma_start(
            out=c.qh[par], in_=c.qdr.ap()[h * 128:h * 128 + 128, :]),
            waits=(("s_st0", 16 * c.TOT_ST[0]), ("s_st1", 16 * c.TOT_ST[1]),
                   ("s_aq", 16 * q_cnt)),
            incs=(("s_aq", 16),))
        q_cnt += 1

        for t in range(NTT):
            for qb in range(NQB_H[h]):
                lo = c_lo(h, t)
                chunks = list(range(lo, 4 * t + 4))
                eoff = EA_OFF[(h, t, qb)]
                per_bank = 4 if qs == 128 else 8
                o_bk = 4 + (qu % 2)
                l_r0 = 64 * (qu % 2)
                ngr = (len(chunks) + per_bank - 1) // per_bank
                gfirst = grp
                for gi in range(ngr):
                    gch = chunks[gi * per_bank:(gi + 1) * per_bank]
                    sbk = grp % 2
                    for pi, ck in enumerate(gch):
                        T.emit("tensor",
                               lambda e, ck=ck, pi=pi, sbk=sbk, par=par, t=t, qb=qb, qs=qs, h=h:
                               e.matmul(
                                   c.bank(sbk)[:, qs * pi:qs * (pi + 1)],
                                   c.kun[par][:, 128 * ck:128 * (ck + 1)],
                                   c.qh[par][:, 128 * t + qs * qb:128 * t + qs * (qb + 1)],
                                   start=True, stop=True, skip_group_check=True),
                               waits=(("s_akk", 16 * kv_cnt), ("s_aq", 16 * q_cnt),
                                      ("s_aex", max(0, grp - 1))) + c.PREATT_GATE,
                               incs=(("s_asm", 1),) if pi == len(gch) - 1 else ())
                    SM = T.v("s_asm")
                    w = qs * len(gch)
                    T.emit("scalar", lambda e, sbk=sbk, w=w, par2=grp % 2: e.activation(
                        c.ering[par2][:, 0:w], c.bank(sbk, w=w), AF.Exp),
                        waits=(("s_asm", SM), ("s_apt", max(0, grp - 1)),
                               ("s_mix", c.MIX)),
                        incs=(("s_aex", 1),))
                    AEX = T.v("s_aex")
                    for pi, ck in enumerate(gch):
                        rel = ck - 4 * t
                        mo = mask_off(rel, qs, qb)
                        msk = c.onesb.broadcast_to([128, qs]) if mo is None \
                            else c.masks[:, mo:mo + qs]
                        T.emit("vector",
                               lambda e, pi=pi, ck=ck, par2=grp % 2, qs=qs,
                               eo=eoff + (ck - lo), msk=msk:
                               e.scalar_tensor_tensor(
                                   c.pring[par2][:, qs * pi:qs * (pi + 1)],
                                   c.ering[par2][:, qs * pi:qs * (pi + 1)],
                                   c.ea_f[:, eo:eo + 1], msk,
                                   ALU.mult, ALU.mult),
                               waits=(("s_aex", AEX), ("s_ea", EA),
                                      ("s_mk", 16), ("s_ape", max(0, grp - 1))),
                               incs=(("s_apt", 1),) if pi == len(gch) - 1 else ())
                    PT = T.v("s_apt")
                    for pi, ck in enumerate(gch):
                        cidx = gi * per_bank + pi
                        is_first = (cidx == 0)
                        is_last = (cidx == len(chunks) - 1)
                        T.emit("tensor",
                               lambda e, ck=ck, pi=pi, par=par, par2=grp % 2, qs=qs,
                               o_bk=o_bk, st=is_first, sp=is_last:
                               e.matmul(
                                   c.bank(o_bk, w=qs),
                                   c.vun[par][:, 128 * ck:128 * (ck + 1)],
                                   c.pring[par2][:, qs * pi:qs * (pi + 1)],
                                   start=st, stop=sp, skip_group_check=True),
                               waits=(("s_apt", PT), ("s_akv", 16 * kv_cnt),
                                      ("s_aoe", max(0, qu - 1)),))
                        T.emit("tensor",
                               lambda e, ck=ck, pi=pi, par2=grp % 2, qs=qs,
                               l_r0=l_r0, st=is_first, sp=is_last:
                               e.matmul(
                                   c.bank(6, w=qs, r0=l_r0, rn=1),
                                   c.onesb,
                                   c.pring[par2][:, qs * pi:qs * (pi + 1)],
                                   start=st, stop=sp, skip_group_check=True),
                               waits=(("s_alr", max(0, qu - 1)),),
                               incs=(("s_ape", 1),) if pi == len(gch) - 1 else ())
                    grp += 1
                APE = T.v("s_ape")
                T.emit("vector", lambda e, l_r0=l_r0, qs=qs: e.reciprocal(
                    c.rlq[0:1, 0:qs], c.bank(6, w=qs, r0=l_r0, rn=1)),
                    waits=(("s_ape", APE),), incs=(("s_alr", 1),))
                ALR = T.v("s_alr")
                T.emit("tensor", lambda e, qs=qs: e.matmul(
                    c.bank(7, w=qs), c.onesf, c.rlq[0:1, 0:qs],
                    start=True, stop=True, skip_group_check=True),
                    waits=(("s_alr", ALR), ("s_arb", max(0, qu))),
                    incs=(("s_arm", 1),))
                ARM = T.v("s_arm")
                T.emit("vector", lambda e, qs=qs: e.tensor_copy(
                    c.rbsb[:, 0:qs], c.bank(7, w=qs)),
                    waits=(("s_arm", ARM),), incs=(("s_arb", 1),))
                ARB = T.v("s_arb")
                dst0 = h * 1024 + 128 * t + qs * qb
                T.emit("vector", lambda e, o_bk=o_bk, qs=qs, dst0=dst0: e.tensor_tensor(
                    c.bigA[:, dst0:dst0 + qs], c.bank(o_bk, w=qs), c.rbsb[:, 0:qs],
                    ALU.mult),
                    waits=(("s_arb", ARB),),
                    incs=(("s_aoe", 1),))
                qu += 1
        pe_grp_after_head.append(grp)
    c.ATT_GRP = grp
    c.ATT_QU = qu
    c.APE_ALL = T.v("s_ape")
    c.AOE_ALL = T.v("s_aoe")
    c.AEX_ALL = T.v("s_aex")


def _emit_wout(c):
    T = c.T
    for j in range(8):
        T.emit("sync", lambda e, j=j: e.dma_start(out=c.rowb, in_=c.bo_d[j:j + 1, :]),
               waits=(("s_bob", j), ("s_ape", c.APE_ALL),
                      ("s_aoe", c.AOE_ALL), ("s_alr", c.ATT_QU)),
               incs=(("s_row", 16),))
        RW = T.v("s_row")
        T.emit("tensor", lambda e: e.matmul(
            c.bank(7), c.onesf, c.rowb, start=True, stop=True,
            skip_group_check=True),
            waits=(("s_row", RW), ("s_bob", j), ("s_arb", c.ATT_QU)),
            incs=(("s_bom", 1),))
        BM = T.v("s_bom")
        T.emit("vector", lambda e, j=j: e.tensor_copy(
            c.bob[:, 512 * j:512 * (j + 1)], c.bank(7)),
            waits=(("s_bom", BM),), incs=(("s_bob", 1),))
    BOB = T.v("s_bob")

    def vstep(fn, extra=()):
        T.emit("vector", fn, waits=tuple(extra) + (("s_tv", T.v("s_tv")),),
               incs=(("s_tv", 1),))

    seq = 0
    c.out_lane = [0, 0]
    c.ax_lane = [0, 0]
    groups_after_wb = []
    for wb in range(32):
        par = wb % 2
        if wb < 2:
            gate = [("s_ape", c.APE_ALL)]
        else:
            gate = [("s_gmm", groups_after_wb[wb - 2])]
        wsem = f"s_w{par}"
        T.emit("gpsimd", lambda e, par=par, wb=wb: e.dma_start(
            out=c.band[par], in_=c.wout_d[wb]),
            waits=tuple(gate), incs=((wsem, 16),))
        WB = T.v(wsem)
        for tt in range(NTT):
            def mk_mm(e, i, bk, par=par, tt=tt):
                return e.matmul(
                    c.bank(bk, w=128),
                    c.bigA[:, 1024 * i + 128 * tt:1024 * i + 128 * (tt + 1)],
                    c.band[par][:, 128 * i:128 * (i + 1)],
                    start=(i == 0), stop=(i == 31), skip_group_check=True)
            def mk_ev(e, bk, wb=wb):
                return e.tensor_tensor(
                    c.ot1, c.bank(bk, w=128), c.bob[:, 128 * wb:128 * (wb + 1)],
                    ALU.add)
            _mm_group(c, 32, mk_mm, "vector", mk_ev,
                      extra_mm_waits=((wsem, WB), ("s_aoe", c.AOE_ALL),
                                      ("s_aex", c.AEX_ALL)),
                      extra_evac_waits=(("s_bob", BOB),))
            GEV1 = T.v("s_gevv")
            axs = f"s_ax{seq % 2}"
            T.emit("sync", lambda e, wb=wb, tt=tt, seq=seq: e.dma_start(
                out=c.xcol[seq % 2],
                in_=c.x_d[128 * tt:128 * (tt + 1), 128 * wb:128 * (wb + 1)]),
                waits=(("s_wev", max(0, seq - 1)),), incs=((axs, 16),))
            AXV = T.v(axs)
            T.emit("vector", lambda e, seq=seq: e.tensor_add(
                c.ot2, c.ot1, c.xcol[seq % 2]),
                waits=((axs, AXV), ("s_gevv", GEV1)),
                incs=(("s_wev", 1),))
            WEV1 = T.v("s_wev")
            T.emit("vector", lambda e, seq=seq, wb=wb, tt=tt: e.tensor_tensor(
                c.ostg[seq % 2], c.ot2,
                c.bigB[:, D * tt + 128 * wb:D * tt + 128 * (wb + 1)], ALU.add),
                waits=(("s_wev", WEV1),
                       (f"s_out{seq % 2}", 16 * c.out_lane[seq % 2]),
                       ("s_mix", c.MIX)),
                incs=(("s_w3", 1),))
            W3 = T.v("s_w3")
            T.emit("scalar", lambda e, seq=seq, wb=wb, tt=tt: e.dma_start(
                out=c.out_d[128 * tt:128 * (tt + 1), 128 * wb:128 * (wb + 1)],
                in_=c.ostg[seq % 2]),
                waits=(("s_w3", W3),), incs=((f"s_out{seq % 2}", 16),))
            c.out_lane[seq % 2] += 1
            seq += 1
        groups_after_wb.append(c.g_seq)


def _emit_debug(c):
    T = c.T
    if hasattr(c, "TOT_ST"):
        stw = (("s_st0", 16 * c.TOT_ST[0]), ("s_st1", 16 * c.TOT_ST[1]))
        T.emit("sync", lambda e: e.dma_start(out=c.dbg_q, in_=c.qdr.ap()),
               waits=stw, incs=(("s_outd", 16),))
        T.emit("sync", lambda e: e.dma_start(out=c.dbg_k, in_=c.cck_i.ap()),
               waits=stw, incs=(("s_outd", 16),))
        T.emit("sync", lambda e: e.dma_start(out=c.dbg_v, in_=c.ccv_i.ap()),
               waits=stw, incs=(("s_outd", 16),))
    T.emit("sync", lambda e: e.dma_start(out=c.dbg_bigB, in_=c.bigB),
           waits=(("s_xn", T.v("s_xn")), ("s_mix", T.v("s_mix"))),
           incs=(("s_outd", 16),))
    T.emit("sync", lambda e: e.dma_start(out=c.dbg_bigA, in_=c.bigA),
           waits=(("s_tev", T.v("s_tev")), ("s_aoe", T.v("s_aoe"))),
           incs=(("s_outd", 16),))
    T.emit("sync", lambda e: e.dma_start(out=c.dbg_f32[:, 0:192], in_=c.lgraw),
           waits=getattr(c, "LGR_GATE", ()), incs=(("s_outd", 16),))
    T.emit("sync", lambda e: e.dma_start(out=c.dbg_f32[:, 192:384], in_=c.lgs),
           incs=(("s_outd", 16),))
    T.emit("sync", lambda e: e.dma_start(out=c.dbg_f32[:, 384:512], in_=c.hres),
           waits=(("s_tv", T.v("s_tv")),), incs=(("s_outd", 16),))
    T.emit("sync", lambda e: e.dma_start(out=c.dbg_f32[:, 512:544], in_=c.hpre),
           incs=(("s_outd", 16),))
    T.emit("sync", lambda e: e.dma_start(out=c.dbg_f32[:, 544:576], in_=c.hpost),
           incs=(("s_outd", 16),))
    T.emit("sync", lambda e: e.dma_start(out=c.dbg_f32[:, 576:584], in_=c.rms_r),
           incs=(("s_outd", 16),))


def kernel(**inputs):
    shared, per_core = prep_inputs(inputs)
    nc = build_program()
    in_maps = [dict(shared, **pc) for pc in per_core]
    res = run_bass_kernel_spmd(nc, in_maps, core_ids=list(range(N_CORES)))
    out = np.zeros((B, S, D), dtype=np.float32)
    for cc in range(N_CORES):
        b, ci = cc // 4, cc % 4
        oc = res.results[cc]["out_core"]
        for t in range(NTT):
            out[b, 128 * (4 * t + ci):128 * (4 * t + ci) + 128, :] = \
                oc[128 * t:128 * (t + 1), :]
    return out


# version 09 (final)
# speedup vs baseline: 2.1600x; 1.0086x over previous
import math
import numpy as np
import ml_dtypes
import concourse.bass as bass
import concourse.mybir as mybir
from concourse.bass_utils import run_bass_kernel_spmd

F32 = mybir.dt.float32
BF16 = mybir.dt.bfloat16
U8 = mybir.dt.uint8
AF = mybir.ActivationFunctionType
ALU = mybir.AluOpType
AX = mybir.AxisListType
BFD = ml_dtypes.bfloat16

B, S, D, H, HD = 2, 4096, 4096, 32, 128
NT, NTT = 1024, 8
N_CORES = 8
EPS_LN = 1e-5
EPS_RMS = 1.1920929e-07
QSCALE = float(np.float32(1.0) / np.float32(math.sqrt(HD)))

_LIMIT = "all"
_DBG = False
_FFN_PART = 1000


def _slopes():
    m = np.arange(1, 33, dtype=np.float32) * np.float32(0.25)
    return (np.float32(1.0) / np.power(np.float32(2.0), m)).astype(np.float32)

SLOPES = _slopes()
ALIBI = SLOPES[:, None] * np.arange(1 - S, 1, dtype=np.float32)[None, :]
QS_H = [64 if h < 4 else 128 for h in range(H)]
NQB_H = [2 if h < 4 else 1 for h in range(H)]
WCH = [max(1, int(math.ceil(math.ceil(88.0 / float(SLOPES[h])) / 128.0)))
       for h in range(H)]

def c_lo(h, t):
    return max(0, 4 * t - WCH[h])

ATT_UNITS = []
EA_OFF = {}
_nea = 0
for _h in range(H):
    for _t in range(NTT):
        for _qb in range(NQB_H[_h]):
            ch = list(range(c_lo(_h, _t), 4 * _t + 4))
            ATT_UNITS.append((_h, _t, _qb, ch))
            EA_OFF[(_h, _t, _qb)] = _nea
            _nea += len(ch)
NEA = _nea
assert NEA <= 4096, NEA

def ea_table(ci):
    tab = np.zeros((128, NEA), dtype=np.float32)
    p = np.arange(128)
    for (h, t, qb, ch) in ATT_UNITS:
        qs = QS_H[h]
        q_last = 128 * (4 * t + ci) + (qb + 1) * qs - 1
        C = ALIBI[h, q_last]
        off = EA_OFF[(h, t, qb)]
        for i, c in enumerate(ch):
            a = ALIBI[h, 128 * c + p]
            tab[:, off + i] = np.exp(np.minimum(a - C, np.float32(0.0)))
    return tab

def mask_table(ci):
    tab = np.zeros((128, 1024), dtype=np.float32)
    tk = np.arange(128)[:, None]
    j128 = np.arange(128)[None, :]
    j64 = np.arange(64)[None, :]
    for rel in range(4):
        tab[:, rel * 128:(rel + 1) * 128] = (tk + 128 * (rel - ci) <= j128)
        for qb in range(2):
            base = 512 + (rel * 2 + qb) * 64
            tab[:, base:base + 64] = (tk + 128 * (rel - ci) - 64 * qb <= j64)
    return tab

def mask_off(rel, qs, qb):
    if rel < 0:
        return None
    return rel * 128 if qs == 128 else 512 + (rel * 2 + qb) * 64


def prep_inputs(inp):
    x = np.asarray(inp["L0_attn_x"], dtype=np.float32)
    w_qkv = np.asarray(inp["L0_attn_w_qkv"], dtype=np.float32)
    b_qkv = np.asarray(inp["L0_attn_b_qkv"], dtype=np.float32)
    w_out = np.asarray(inp["L0_attn_w_out"], dtype=np.float32)
    b_out = np.asarray(inp["L0_attn_b_out"], dtype=np.float32)
    W_res = np.asarray(inp["L0_ffn_W_res"], dtype=np.float32)
    W_pre = np.asarray(inp["L0_ffn_W_pre"], dtype=np.float32)
    W_post = np.asarray(inp["L0_ffn_W_post"], dtype=np.float32)
    H_res_b = np.asarray(inp["L0_ffn_H_res_base"], dtype=np.float32)
    H_pre_b = np.asarray(inp["L0_ffn_H_pre_base"], dtype=np.float32)
    H_post_b = np.asarray(inp["L0_ffn_H_post_base"], dtype=np.float32)
    gate = np.asarray(inp["L0_ffn_gate"], dtype=np.float32)
    W_branch = np.asarray(inp["L0_ffn_W_branch"], dtype=np.float32)

    wqkvT = np.ascontiguousarray(w_qkv.T)
    wqk = wqkvT[:, :8192].reshape(32, 128, 64, 128).transpose(2, 1, 0, 3)
    wqk_tiled = np.ascontiguousarray(wqk).reshape(64, 128, 4096)
    wv = wqkvT[:, 8192:].reshape(32, 128, 32, 128).transpose(2, 1, 0, 3)
    wv_tiled = np.ascontiguousarray(wv).reshape(32, 128, 4096)
    woT = np.ascontiguousarray(w_out.T)
    wo = woT.reshape(32, 128, 32, 128).transpose(2, 1, 0, 3)
    wout_tiled = np.ascontiguousarray(wo).reshape(32, 128, 4096)
    wbr = W_branch.reshape(8, 128, 8, 128).transpose(1, 2, 0, 3)
    wbr_tiled = np.ascontiguousarray(wbr).reshape(128, 8192)
    wsm = np.concatenate([W_res.T, W_pre.T, W_post.T], axis=1)
    wsm_tiled = np.ascontiguousarray(
        wsm.reshape(32, 128, 24).transpose(1, 0, 2)).reshape(128, 768)
    bias_pf = np.ascontiguousarray(b_qkv.reshape(96, 128).T)
    hsmall = np.concatenate([H_res_b.reshape(-1), H_pre_b, H_post_b, gate])
    shared = dict(
        wqk_tiled=wqk_tiled, wv_tiled=wv_tiled, wout_tiled=wout_tiled,
        wbr_tiled=wbr_tiled, wsm_tiled=wsm_tiled, bias_pf=bias_pf,
        b_v=np.ascontiguousarray(b_qkv[8192:]).reshape(8, 512),
        b_out=np.ascontiguousarray(b_out).reshape(8, 512),
        hsmall=hsmall.astype(np.float32).reshape(1, 25),
    )
    per_core = []
    for c in range(N_CORES):
        b, ci = c // 4, c % 4
        xs = np.concatenate(
            [x[b, 128 * (4 * t + ci):128 * (4 * t + ci) + 128, :]
             for t in range(NTT)], axis=0)
        per_core.append(dict(x_core=np.ascontiguousarray(xs),
                             masks01=mask_table(ci), ea_tab=ea_table(ci)))
    return shared, per_core


ENGS = ("sync", "gpsimd", "tensor", "vector", "scalar")

class Tape:
    def __init__(self, nc):
        self.nc = nc
        self.ops = {e: [] for e in ENGS}
        self.sems = {}
        self.cnt = {}

    def sem(self, name):
        if name not in self.sems:
            self.sems[name] = self.nc.alloc_semaphore(name)
            self.cnt[name] = 0
        return self.sems[name]

    def emit(self, eng, fn, waits=(), incs=()):
        rw = [(self.sem(s), int(val)) for s, val in waits if int(val) > 0]
        ri = []
        for s, n in incs:
            ri.append((self.sem(s), n))
            self.cnt[s] += n
        self.ops[eng].append((rw, fn, ri))

    def wait(self, eng, name, val):
        if int(val) > 0:
            self.ops[eng].append(([(self.sem(name), int(val))], None, []))

    def v(self, name):
        self.sem(name)
        return self.cnt[name]

    def replay(self, eng, engine):
        seen = {}
        for rw, fn, ri in self.ops[eng]:
            for s, val in rw:
                if seen.get(id(s), -1) >= val:
                    continue
                engine.wait_ge(s, val)
                seen[id(s)] = val
            if fn is None:
                continue
            ins = fn(engine)
            for s, n in ri:
                ins.then_inc(s, n)


class Ctx:
    pass


def build_program():
    nc = bass.Bass("TRN2", target_bir_lowering=False,
                   dynamic_dma_scratch_size=4096)
    c = Ctx()
    c.nc = nc

    c.x_d = nc.dram_tensor("x_core", [NT, D], F32, kind="ExternalInput").ap()
    c.wqk_d = nc.dram_tensor("wqk_tiled", [64, 128, 4096], F32, kind="ExternalInput").ap()
    c.wv_d = nc.dram_tensor("wv_tiled", [32, 128, 4096], F32, kind="ExternalInput").ap()
    c.wout_d = nc.dram_tensor("wout_tiled", [32, 128, 4096], F32, kind="ExternalInput").ap()
    c.wbr_d = nc.dram_tensor("wbr_tiled", [128, 8192], F32, kind="ExternalInput").ap()
    c.wsm_d = nc.dram_tensor("wsm_tiled", [128, 768], F32, kind="ExternalInput").ap()
    c.bias_d = nc.dram_tensor("bias_pf", [128, 96], F32, kind="ExternalInput").ap()
    c.bv_d = nc.dram_tensor("b_v", [8, 512], F32, kind="ExternalInput").ap()
    c.bo_d = nc.dram_tensor("b_out", [8, 512], F32, kind="ExternalInput").ap()
    c.hs_d = nc.dram_tensor("hsmall", [1, 25], F32, kind="ExternalInput").ap()
    c.mk_d = nc.dram_tensor("masks01", [128, 1024], F32, kind="ExternalInput").ap()
    c.ea_d = nc.dram_tensor("ea_tab", [128, NEA], F32, kind="ExternalInput").ap()
    c.out_d = nc.dram_tensor("out_core", [NT, D], F32, kind="ExternalOutput").ap()

    c.qdr = nc.dram_tensor("q_scr", [4096, 1024], BF16)
    c.cck_i = nc.dram_tensor("cc_in_k", [4096, 1024], BF16)
    c.ccv_i = nc.dram_tensor("cc_in_v", [H * 1024, 128], BF16)
    c.cck_o = nc.dram_tensor("cc_out_k", [4 * 4096, 1024], BF16)
    c.ccv_o = nc.dram_tensor("cc_out_v", [4 * H * 1024, 128], BF16)

    c.ident_d = nc.inline_tensor(np.eye(128, dtype=BFD), "ident_d")
    c.i24_d = nc.inline_tensor(np.eye(24, dtype=np.float32), "i24_d")
    c.onesf_d = nc.inline_tensor(np.ones((1, 128), dtype=np.float32), "onesf_d")
    c.onesb_d = nc.inline_tensor(np.ones((128, 1), dtype=BFD), "onesb_d")
    c.eps_d = nc.inline_tensor(
        np.concatenate([np.full((128, 1), EPS_LN, dtype=np.float32),
                        np.full((128, 1), EPS_RMS, dtype=np.float32)], axis=1),
        "eps_d")

    if _DBG:
        c.dbg_bigB = nc.dram_tensor("dbg_bigB", [128, 8 * D], BF16, kind="ExternalOutput").ap()
        c.dbg_bigA = nc.dram_tensor("dbg_bigA", [128, 32 * NT], BF16, kind="ExternalOutput").ap()
        c.dbg_f32 = nc.dram_tensor("dbg_f32", [128, 1024], F32, kind="ExternalOutput").ap()
        c.dbg_q = nc.dram_tensor("dbg_q", [4096, 1024], BF16, kind="ExternalOutput").ap()
        c.dbg_k = nc.dram_tensor("dbg_k", [4096, 1024], BF16, kind="ExternalOutput").ap()
        c.dbg_v = nc.dram_tensor("dbg_v", [H * 1024, 128], BF16, kind="ExternalOutput").ap()

    def arena(name, nbytes):
        return nc.alloc_sbuf_tensor(name, [128, nbytes], U8)[:]
    R2 = arena("R2", 16384)
    R3 = arena("R3", 16384)
    R4 = arena("R4", 16384)
    R5 = arena("R5", 8192)
    R7 = arena("R7", 8192)
    R8 = arena("R8", 4096)

    c.xt = R2[:, 0:16384].bitcast(F32)
    c.wbr_bf = R2[:, 0:16384].bitcast(BF16)
    c.ea_f = R2[:, 0:4 * NEA].bitcast(F32)
    c.band = [R3[:, 0:8192].bitcast(BF16), R4[:, 0:8192].bitcast(BF16)]
    c.xpreT = R3[:, 0:16384].bitcast(BF16)
    c.juggle = R4[:, 0:16384].bitcast(BF16)
    c.kun = [R3[:, 0:8192].bitcast(BF16), R4[:, 0:8192].bitcast(BF16)]
    c.vun = [R3[:, 8192:16384].bitcast(BF16), R4[:, 8192:16384].bitcast(BF16)]
    c.ytok = R3[:, 0:16384].bitcast(BF16)
    c.scrbf = R5[:, 0:8192].bitcast(BF16)
    c.ering = [R5[:, 0:1024].bitcast(BF16), R5[:, 1024:2048].bitcast(BF16)]
    c.pring = [R5[:, 2048:3072].bitcast(BF16), R5[:, 3072:4096].bitcast(BF16)]
    c.rbsb = R5[:, 4096:6144].bitcast(F32)
    c.rlq = R5[:, 6144:8192].bitcast(F32)
    c.lgf = R5[:, 0:4096].bitcast(F32)[0:24, :]
    c.rowb = R5[0:1, 4096:6144].bitcast(F32)
    c.stage = [R8[:, 0:1024].bitcast(BF16), R8[:, 1024:2048].bitcast(BF16)]
    c.xpre = R8[:, 0:4096].bitcast(F32)
    c.bvb = R7[:, 0:8192].bitcast(BF16)
    c.bob = R7[:, 0:8192].bitcast(BF16)

    al = nc.alloc_sbuf_tensor
    c.bigA = al("bigA", [128, 32 * NT], BF16)[:]
    c.bigB = al("bigB", [128, 8 * D], BF16)[:]
    c.masks = al("masks", [128, 1024], BF16)[:]
    c.onesb = al("onesb", [128, 1], BF16)[:]
    c.ident = al("ident", [128, 128], BF16)[:]
    c.i24 = al("i24", [24, 24], F32)[:]
    c.onesf = al("onesf", [1, 128], F32)[:]
    c.bias_sb = al("bias_sb", [128, 96], F32)[:]
    c.bias_q = al("bias_q", [128, 32], F32)[:]
    c.hs = al("hs", [1, 32], F32)[:]
    c.base24 = al("base24", [128, 24], F32)[:]
    c.wsm_bf = al("wsm_bf", [128, 768], BF16)[:]
    c.lgraw = al("lgraw", [128, 192], F32)[:]
    c.lgs = al("lgs", [128, 192], F32)[:]
    c.zbuf = al("zbuf", [128, 128], F32)[:]
    c.abuf = al("abuf", [128, 128], F32)[:]
    c.redu = al("redu", [128, 32], F32)[:]
    c.lnsu = al("lnsu", [128, 32], F32)[:]
    c.lnsv = al("lnsv", [128, 32], F32)[:]
    c.hres = al("hres", [128, 128], F32)[:]
    c.hpre = al("hpre", [128, 32], F32)[:]
    c.hpost = al("hpost", [128, 32], F32)[:]
    c.gcol = al("gcol", [128, 2], F32)[:]
    c.epsc = al("epsc", [128, 2], F32)[:]
    c.g1 = al("g1", [1, 3], F32)[:]
    c.stat = al("stat", [128, 16], F32)[:]
    c.rms_r = al("rms_r", [128, 8], F32)[:]
    c.qh = [al(f"qh{i}", [128, 1024], BF16)[:] for i in range(2)]
    c.ot1 = al("ot1", [128, 128], F32)[:]
    c.ot2 = al("ot2", [128, 128], F32)[:]
    c.xcol = [al(f"xcol{i}", [128, 128], F32)[:] for i in range(2)]
    c.ostg = [al(f"ostg{i}", [128, 128], F32)[:] for i in range(2)]

    c.ps = nc.alloc_psum_tensor("pspool", [128, 4096], F32)[:]
    c.bank = lambda i, w=512, r0=0, rn=128: c.ps[r0:r0 + rn, 512 * i:512 * i + w]

    T = Tape(nc)
    c.T = T

    c.g_seq = 0
    c.evac_info = []

    _emit_preload(c)
    _emit_p0(c)
    _emit_bvb(c)
    if _LIMIT != "p0":
        _emit_p1(c)
    if _LIMIT not in ("p0", "p1"):
        _emit_ffn(c)
    if _LIMIT not in ("p0", "p1", "ffn"):
        _emit_attention(c)
    if _LIMIT == "all":
        _emit_wout(c)
    if _DBG:
        _emit_debug(c)

    T.wait("sync", "s_out0", T.v("s_out0"))
    T.wait("sync", "s_out1", T.v("s_out1"))
    T.wait("sync", "s_outd", T.v("s_outd"))

    with nc.Block() as block:
        @block.sync
        def _(sy):
            T.replay("sync", sy)

        @block.gpsimd
        def _(g):
            T.replay("gpsimd", g)

        @block.tensor
        def _(t):
            T.replay("tensor", t)

        @block.vector
        def _(v):
            T.replay("vector", v)

        @block.scalar
        def _(s):
            T.replay("scalar", s)
    return nc


def _emit_preload(c):
    T = c.T
    for dst, srcap in [(c.onesb, c.onesb_d.ap()), (c.epsc, c.eps_d.ap()),
                       (c.ident, c.ident_d.ap()), (c.i24, c.i24_d.ap()),
                       (c.onesf, c.onesf_d.ap()), (c.bias_sb, c.bias_d),
                       (c.hs[0:1, 0:25], c.hs_d)]:
        T.emit("sync", lambda e, d=dst, s=srcap: e.dma_start(out=d, in_=s),
               incs=(("s_pre", 16),))
    T.emit("gpsimd", lambda e: e.dma_start(out=c.masks, in_=c.mk_d),
           incs=(("s_mk", 16),))
    c.PRE = T.v("s_pre")
    T.emit("vector", lambda e: e.tensor_scalar_mul(c.bias_q, c.bias_sb[:, 0:32], QSCALE),
           waits=(("s_pre", c.PRE),), incs=(("s_bq", 1),))


def _emit_bvb(c):
    T = c.T
    for j in range(8):
        T.emit("sync", lambda e, j=j: e.dma_start(out=c.rowb, in_=c.bv_d[j:j + 1, :]),
               waits=(("s_bvb", j),), incs=(("s_row", 16),))
        T.emit("tensor", lambda e, j=j: e.matmul(
            c.bank(6 + j % 2), c.onesf, c.rowb, start=True, stop=True,
            skip_group_check=True),
            waits=(("s_row", 16 * (j + 1)), ("s_bvb", j)), incs=(("s_bvm", 1),))
        T.emit("vector", lambda e, j=j: e.tensor_copy(
            c.bvb[:, 512 * j:512 * (j + 1)], c.bank(6 + j % 2)),
            waits=(("s_bvm", j + 1),), incs=(("s_bvb", 1),))
    c.BVB = T.v("s_bvb")


def _emit_p0(c):
    T = c.T
    ps = c.ps
    tgrp = 0
    for tc in range(NTT):
        w = [("s_xn", tc)] if tc > 0 else []
        T.emit("sync", lambda e, tc=tc: e.dma_start(
            out=c.xt, in_=c.x_d[128 * tc:128 * (tc + 1), :]),
            waits=tuple(w), incs=(("s_x", 16),))
        XV = T.v("s_x")
        T.emit("vector", lambda e: e.reduce_sum(c.stat[:, 0:1], c.xt, AX.X),
               waits=(("s_x", XV),), incs=(("s_sum", 1),))
        T.emit("scalar", lambda e: e.activation(
            ps[:, 1024:3072], c.xt[:, 0:2048], AF.Square, accum_out=c.stat[:, 1:2]),
            waits=(("s_x", XV),), incs=(("s_ta", 1),))
        T.emit("scalar", lambda e: e.activation(
            ps[:, 1024:3072], c.xt[:, 2048:4096], AF.Square, accum_out=c.stat[:, 2:3]),
            waits=(("s_ta", T.v("s_ta")),), incs=(("s_sq", 1),))
        SQ, SM = T.v("s_sq"), T.v("s_sum")

        def vstep(fn, extra=()):
            T.emit("vector", fn, waits=tuple(extra) + (("s_tv", T.v("s_tv")),),
                   incs=(("s_tv", 1),))

        def astep(fn, extra=()):
            T.emit("scalar", fn, waits=tuple(extra) + (("s_ta", T.v("s_ta")),),
                   incs=(("s_ta", 1),))

        vstep(lambda e: e.tensor_add(c.stat[:, 3:4], c.stat[:, 1:2], c.stat[:, 2:3]),
              ((("s_sq"), SQ),))
        vstep(lambda e: e.tensor_scalar_mul(c.stat[:, 4:5], c.stat[:, 0:1], 1.0 / D),
              ((("s_sum"), SM),))
        vstep(lambda e: e.tensor_scalar_mul(c.stat[:, 5:6], c.stat[:, 3:4], 1.0 / D))
        vstep(lambda e: e.tensor_mul(c.stat[:, 6:7], c.stat[:, 4:5], c.stat[:, 4:5]))
        vstep(lambda e: e.tensor_sub(c.stat[:, 7:8], c.stat[:, 5:6], c.stat[:, 6:7]))
        TV = T.v("s_tv")
        astep(lambda e: e.activation(c.stat[:, 8:9], c.stat[:, 7:8], AF.Sqrt,
                                     bias=c.epsc[:, 0:1]), ((("s_tv"), TV),))
        TA = T.v("s_ta")
        vstep(lambda e: e.reciprocal(c.stat[:, 9:10], c.stat[:, 8:9]),
              ((("s_ta"), TA),))
        vstep(lambda e: e.tensor_mul(c.stat[:, 11:12], c.stat[:, 7:8], c.stat[:, 9:10]))
        vstep(lambda e: e.tensor_mul(c.stat[:, 12:13], c.stat[:, 11:12], c.stat[:, 9:10]))
        TV = T.v("s_tv")
        astep(lambda e: e.activation(c.stat[:, 13:14], c.stat[:, 12:13], AF.Sqrt,
                                     bias=c.epsc[:, 1:2]), ((("s_tv"), TV),))
        TA = T.v("s_ta")
        vstep(lambda e, tc=tc: e.reciprocal(c.rms_r[:, tc:tc + 1], c.stat[:, 13:14]),
              ((("s_ta"), TA),))
        TV = T.v("s_tv")
        T.emit("vector", lambda e, tc=tc: e.tensor_scalar(
            c.bigB[:, D * tc:D * (tc + 1)], c.xt, c.stat[:, 4:5], c.stat[:, 9:10],
            ALU.subtract, ALU.mult),
            waits=(("s_tv", TV),), incs=(("s_xn", 1),))
        XN = T.v("s_xn")
        for g in range(8):
            bk = tgrp % 2
            for q in range(4):
                dc = 4 * g + q
                incs = (("s_tmm", 1),) if q == 3 else ()
                T.emit("tensor", lambda e, tc=tc, dc=dc, bk=bk, q=q: e.matmul(
                    c.bank(bk)[:, 128 * q:128 * (q + 1)],
                    c.bigB[:, D * tc + 128 * dc:D * tc + 128 * (dc + 1)], c.ident,
                    start=True, stop=True, skip_group_check=True),
                    waits=(("s_xn", XN), ("s_pre", c.PRE),
                           ("s_tev", max(0, tgrp - 1))),
                    incs=incs)
            tgrp += 1
            TM = T.v("s_tmm")
            T.emit("scalar", lambda e, tc=tc, g=g, bk=bk: e.activation(
                c.bigA.rearrange("p (c t) -> p c t", t=NT)
                    [:, 4 * g:4 * g + 4, 128 * tc:128 * (tc + 1)],
                c.bank(bk).rearrange("p (c t) -> p c t", t=128),
                AF.Identity),
                waits=(("s_tmm", TM),), incs=(("s_tev", 1),))
    c.P0_XN = T.v("s_xn")
    c.P0_TEV = T.v("s_tev")


def gev_now(c):
    w = []
    for s in ("s_geva", "s_gevv"):
        v = c.T.v(s)
        if v:
            w.append((s, v))
    return tuple(w)


def _mm_group(c, n_mm, mk_mm, evac_eng, mk_evac, extra_mm_waits=(),
              extra_evac_waits=()):
    T = c.T
    g = c.g_seq
    bk = g % 4
    c.g_seq += 1
    war = (c.evac_info[g - 4],) if g >= 4 else ()
    for i in range(n_mm):
        incs = (("s_gmm", 1),) if i == n_mm - 1 else ()
        T.emit("tensor", lambda e, i=i, bk=bk: mk_mm(e, i, bk),
               waits=tuple(extra_mm_waits) + war,
               incs=incs)
    GM = T.v("s_gmm")
    esem = "s_geva" if evac_eng == "scalar" else "s_gevv"
    T.emit(evac_eng, lambda e, bk=bk: mk_evac(e, bk),
           waits=tuple(extra_evac_waits) + (("s_gmm", GM),),
           incs=((esem, 1),))
    c.evac_info.append((esem, T.v(esem)))
    return g


def _emit_p1(c):
    T = c.T
    band_plan = ([("K", i) for i in range(32, 64)] +
                 [("V", i) for i in range(32)] +
                 [("Q", i) for i in range(32)])
    groups_after_band = []
    nst = 0
    c.st_lane = [0, 0]
    c.k_store = None
    c.v_store = None
    for bi, (kind, idx) in enumerate(band_plan):
        par = bi % 2
        src = c.wqk_d if kind in ("K", "Q") else c.wv_d
        si = idx if kind != "V" else idx
        w = [] if bi < 2 else [("s_gmm", groups_after_band[bi - 2])]
        wsem = f"s_w{par}"
        T.emit("gpsimd", lambda e, par=par, src=src, si=si: e.dma_start(
            out=c.band[par], in_=src[si]),
            waits=tuple(w), incs=((wsem, 16),))
        WB = T.v(wsem)
        if kind in ("K", "Q"):
            for tb in range(2):
                def mk_mm(e, i, bk, par=par, tb=tb, WB=WB):
                    return e.matmul(
                        c.bank(bk), c.band[par][:, 128 * i:128 * (i + 1)],
                        c.bigA[:, 1024 * i + 512 * tb:1024 * i + 512 * (tb + 1)],
                        start=(i == 0), stop=(i == 31), skip_group_check=True)
                stg = c.stage[nst % 2]
                if kind == "Q":
                    def mk_ev(e, bk, stg=stg, idx=idx):
                        return e.activation(stg, c.bank(bk), AF.Identity,
                                            bias=c.bias_q[:, idx:idx + 1],
                                            scale=QSCALE)
                    ew = [("s_bq", 1)]
                else:
                    def mk_ev(e, bk, stg=stg, idx=idx):
                        return e.activation(stg, c.bank(bk), AF.Identity,
                                            bias=c.bias_sb[:, idx:idx + 1])
                    ew = []
                ew.append((f"s_st{nst % 2}", 16 * c.st_lane[nst % 2]))
                _mm_group(c, 32, mk_mm, "scalar", mk_ev,
                          extra_mm_waits=((wsem, WB), ("s_tev", c.P0_TEV)),
                          extra_evac_waits=tuple(ew))
                EV = c.evac_info[-1]
                dst = c.cck_i if kind == "K" else c.qdr
                r0 = (idx - 32) * 128 if kind == "K" else idx * 128
                T.emit("sync", lambda e, dst=dst, r0=r0, tb=tb, stg=stg: e.dma_start(
                    out=dst.ap()[r0:r0 + 128, 512 * tb:512 * (tb + 1)], in_=stg),
                    waits=(EV,), incs=((f"s_st{nst % 2}", 16),))
                c.st_lane[nst % 2] += 1
                nst += 1
        else:
            for tc in range(NTT):
                def mk_mm(e, i, bk, par=par, tc=tc):
                    return e.matmul(
                        c.bank(bk, w=128),
                        c.bigA[:, 1024 * i + 128 * tc:1024 * i + 128 * (tc + 1)],
                        c.band[par][:, 128 * i:128 * (i + 1)],
                        start=(i == 0), stop=(i == 31), skip_group_check=True)
                stg = c.stage[nst % 2]
                def mk_ev(e, bk, stg=stg, idx=idx):
                    return e.tensor_tensor(stg[:, 0:128], c.bank(bk, w=128),
                                           c.bvb[:, 128 * idx:128 * (idx + 1)],
                                           ALU.add)
                _mm_group(c, 32, mk_mm, "vector", mk_ev,
                          extra_mm_waits=((wsem, WB), ("s_tev", c.P0_TEV)),
                          extra_evac_waits=(("s_bvb", c.BVB),
                                            (f"s_st{nst % 2}", 16 * c.st_lane[nst % 2])))
                EV = c.evac_info[-1]
                r0 = idx * 1024 + tc * 128
                T.emit("sync", lambda e, r0=r0, stg=stg: e.dma_start(
                    out=c.ccv_i.ap()[r0:r0 + 128, :], in_=stg[:, 0:128]),
                    waits=(EV,), incs=((f"s_st{nst % 2}", 16),))
                c.st_lane[nst % 2] += 1
                nst += 1
        groups_after_band.append(c.g_seq)
        if kind == "K" and idx == 63:
            c.k_store = tuple(c.st_lane)
        if kind == "V" and idx == 31:
            c.v_store = tuple(c.st_lane)

    T.emit("gpsimd", lambda e: e.collective_compute(
        "AllGather", ALU.bypass, replica_groups=[[0, 1, 2, 3], [4, 5, 6, 7]],
        ins=[c.cck_i.ap().opt()], outs=[c.cck_o.ap().opt()]),
        waits=(("s_st0", 16 * c.k_store[0]), ("s_st1", 16 * c.k_store[1])),
        incs=(("s_ccK", 1),))
    T.emit("gpsimd", lambda e: e.collective_compute(
        "AllGather", ALU.bypass, replica_groups=[[0, 1, 2, 3], [4, 5, 6, 7]],
        ins=[c.ccv_i.ap().opt()], outs=[c.ccv_o.ap().opt()]),
        waits=(("s_st0", 16 * c.v_store[0]), ("s_st1", 16 * c.v_store[1])),
        incs=(("s_ccV", 1),))

    T.emit("gpsimd", lambda e: e.dma_start(out=c.wsm_bf, in_=c.wsm_d),
           waits=(("s_gmm", groups_after_band[-2]),), incs=(("s_wsm", 16),))
    WSM = 16
    for tb in range(2):
        def mk_mm(e, i, bk, tb=tb):
            return e.matmul(
                c.bank(bk, r0=0, rn=24), c.wsm_bf[:, 24 * i:24 * (i + 1)],
                c.bigA[:, 1024 * i + 512 * tb:1024 * i + 512 * (tb + 1)],
                start=(i == 0), stop=(i == 31), skip_group_check=True)
        def mk_ev(e, bk, tb=tb):
            return e.activation(c.lgf[0:24, 512 * tb:512 * (tb + 1)],
                                c.bank(bk, r0=0, rn=24), AF.Identity)
        _mm_group(c, 32, mk_mm, "scalar", mk_ev,
                  extra_mm_waits=(("s_wsm", WSM),))
    lg_gate = gev_now(c)
    def mk_mm(e, i, bk):
        return e.matmul(
            c.bank(bk)[:, 24 * i:24 * (i + 1)],
            c.lgf[0:24, 128 * i:128 * (i + 1)], c.i24,
            start=True, stop=True, skip_group_check=True)
    def mk_ev(e, bk):
        return e.tensor_copy(c.lgraw, c.bank(bk, w=192))
    _mm_group(c, 8, mk_mm, "vector", mk_ev,
              extra_mm_waits=(("s_pre", c.PRE),) + lg_gate)
    c.LGR_GATE = gev_now(c)
    c.P1_GROUPS = c.g_seq
    c.TOT_ST = tuple(c.st_lane)
    c.BANDS_GMM = groups_after_band[-1]


def _emit_ffn(c):
    T = c.T

    def vstep(fn, extra=()):
        T.emit("vector", fn, waits=tuple(extra) + (("s_tv", T.v("s_tv")),),
               incs=(("s_tv", 1),))

    def astep(fn, extra=()):
        T.emit("scalar", fn, waits=tuple(extra) + (("s_ta", T.v("s_ta")),),
               incs=(("s_ta", 1),))

    _mm_group(c, 1,
              lambda e, i, bk: e.matmul(c.bank(bk, w=24), c.onesf,
                                        c.hs[0:1, 0:24], start=True, stop=True,
                                        skip_group_check=True),
              "vector",
              lambda e, bk: e.tensor_copy(c.base24, c.bank(bk, w=24)),
              extra_mm_waits=(("s_pre", c.PRE),))
    B24_GATE = gev_now(c)
    astep(lambda e: e.activation(c.g1[0:1, 0:1], c.hs[0:1, 24:25], AF.Exp,
                                 scale=-1.0), ((("s_pre"), c.PRE),))
    TA = T.v("s_ta")
    vstep(lambda e: e.tensor_scalar_add(c.g1[0:1, 1:2], c.g1[0:1, 0:1], 1.0),
          ((("s_ta"), TA),))
    vstep(lambda e: e.reciprocal(c.g1[0:1, 0:1], c.g1[0:1, 1:2]))
    vstep(lambda e: e.memset(c.g1[0:1, 2:3], 1.0))
    vstep(lambda e: e.tensor_sub(c.g1[0:1, 1:2], c.g1[0:1, 2:3], c.g1[0:1, 0:1]))
    TV = T.v("s_tv")
    _mm_group(c, 1,
              lambda e, i, bk: e.matmul(c.bank(bk, w=2), c.onesf,
                                        c.g1[0:1, 0:2], start=True, stop=True,
                                        skip_group_check=True),
              "vector",
              lambda e, bk: e.tensor_copy(c.gcol, c.bank(bk, w=2)),
              extra_mm_waits=(("s_tv", TV),))
    GC_GATE = gev_now(c)

    for tc in range(NTT):
        vstep(lambda e, tc=tc: e.scalar_tensor_tensor(
            c.lgs[:, 24 * tc:24 * (tc + 1)], c.lgraw[:, 24 * tc:24 * (tc + 1)],
            c.rms_r[:, tc:tc + 1], c.base24, ALU.mult, ALU.add),
            B24_GATE)

    lgs3 = c.lgs.rearrange("p (a c) -> p a c", c=24)
    for (dst, off) in ((c.hpre, 16), (c.hpost, 20)):
        d3 = dst.rearrange("p (a n) -> p a n", n=4)
        vstep(lambda e, off=off: e.reduce_max(c.redu[:, 0:8], lgs3[:, :, off:off + 4], AX.X))
        vstep(lambda e, off=off, d3=d3: e.tensor_tensor(
            d3, lgs3[:, :, off:off + 4],
            c.redu[:, 0:8].rearrange("p (a o) -> p a o", o=1).broadcast_to([128, 8, 4]),
            ALU.subtract))
        TV = T.v("s_tv")
        astep(lambda e, dst=dst: e.activation(dst, dst, AF.Exp), ((("s_tv"), TV),))
        TA = T.v("s_ta")
        vstep(lambda e, d3=d3: e.reduce_sum(c.redu[:, 8:16], d3, AX.X),
              ((("s_ta"), TA),))
        vstep(lambda e: e.reciprocal(c.redu[:, 16:24], c.redu[:, 8:16]))
        vstep(lambda e, d3=d3: e.tensor_tensor(
            d3, d3,
            c.redu[:, 16:24].rearrange("p (a o) -> p a o", o=1).broadcast_to([128, 8, 4]),
            ALU.mult))
    HPP = T.v("s_tv")
    if _FFN_PART < 2:
        c.MIX = 0
        c.YTK_GATE = ()
        c.FFN_G = c.g_seq
        return

    z3 = c.zbuf.rearrange("p (a i j) -> p a i j", i=4, j=4)
    z3j = c.zbuf.rearrange("p (a i j) -> p a j i", i=4, j=4)
    a3 = c.abuf.rearrange("p (a i j) -> p a i j", i=4, j=4)
    a3j = c.abuf.rearrange("p (a i j) -> p a j i", i=4, j=4)
    z2 = c.zbuf.rearrange("p (a e) -> p a e", e=16)
    r_i = lambda buf: buf.rearrange("p (a i) -> p a i", i=4).unsqueeze(3).broadcast_to([128, 8, 4, 4])
    r_j = lambda buf: buf.rearrange("p (a j) -> p a j", j=4).unsqueeze(2).broadcast_to([128, 8, 4, 4])

    vstep(lambda e: e.tensor_scalar_mul(z2, lgs3[:, :, 0:16], 10.0))
    vstep(lambda e: e.tensor_reduce(c.redu[:, 0:32], z3, AX.X, ALU.max))
    vstep(lambda e: e.tensor_tensor(z3, z3, r_i(c.redu[:, 0:32]), ALU.subtract))
    vstep(lambda e: e.tensor_reduce(c.redu[:, 0:32], z3j, AX.X, ALU.max))
    vstep(lambda e: e.tensor_tensor(z3j, z3j, r_i(c.redu[:, 0:32]), ALU.subtract))
    TA_prev = None
    for it in range(50):
        if it == 0:
            vstep(lambda e: e.tensor_scalar_max(c.abuf, c.zbuf, -80.0))
        else:
            vstep(lambda e: e.tensor_tensor(a3, z3, r_j(c.lnsv), ALU.subtract),
                  ((("s_ta"), TA_prev),))
            vstep(lambda e: e.tensor_scalar_max(c.abuf, c.abuf, -80.0))
        TV = T.v("s_tv")
        astep(lambda e: e.activation(c.abuf, c.abuf, AF.Exp), ((("s_tv"), TV),))
        TA = T.v("s_ta")
        vstep(lambda e: e.tensor_reduce(c.redu[:, 0:32], a3, AX.X, ALU.add),
              ((("s_ta"), TA),))
        TV = T.v("s_tv")
        astep(lambda e: e.activation(c.lnsu, c.redu[:, 0:32], AF.Ln), ((("s_tv"), TV),))
        TA = T.v("s_ta")
        vstep(lambda e: e.tensor_tensor(a3, z3, r_i(c.lnsu), ALU.subtract),
              ((("s_ta"), TA),))
        vstep(lambda e: e.tensor_scalar_max(c.abuf, c.abuf, -80.0))
        TV = T.v("s_tv")
        astep(lambda e: e.activation(c.abuf, c.abuf, AF.Exp), ((("s_tv"), TV),))
        TA = T.v("s_ta")
        vstep(lambda e: e.tensor_reduce(c.redu[:, 0:32], a3j, AX.X, ALU.add),
              ((("s_ta"), TA),))
        TV = T.v("s_tv")
        astep(lambda e: e.activation(c.lnsv, c.redu[:, 0:32], AF.Ln), ((("s_tv"), TV),))
        TA_prev = T.v("s_ta")
    TA = T.v("s_ta")
    vstep(lambda e: e.tensor_tensor(a3, z3, r_i(c.lnsu), ALU.subtract),
          ((("s_ta"), TA),))
    vstep(lambda e: e.tensor_tensor(a3, a3, r_j(c.lnsv), ALU.subtract))
    TV = T.v("s_tv")
    astep(lambda e: e.activation(c.hres, c.abuf, AF.Exp), ((("s_tv"), TV),))
    TA = T.v("s_ta")
    vstep(lambda e: e.tensor_scalar_mul(c.hres, c.hres, c.gcol[:, 0:1]),
          ((("s_ta"), TA),) + GC_GATE)
    h16 = c.hres.rearrange("p (a e) -> p a e", e=16)
    for i in range(4):
        vstep(lambda e, i=i: e.tensor_scalar_add(
            h16[:, :, 5 * i:5 * i + 1], h16[:, :, 5 * i:5 * i + 1], c.gcol[:, 1:2]))
    HRES = T.v("s_tv")
    if _FFN_PART < 25:
        c.MIX = 0
        c.YTK_GATE = ()
        c.FFN_G = c.g_seq
        return

    for tc in range(NTT):
        xb = lambda n, tc=tc: c.bigB[:, D * tc + 1024 * n:D * tc + 1024 * (n + 1)]
        vstep(lambda e, tc=tc, xb=xb: e.tensor_scalar_mul(
            c.xpre, xb(0), c.hpre[:, 4 * tc:4 * tc + 1]),
            ((("s_tv"), HPP), (("s_st0"), 16 * c.TOT_ST[0]),
             (("s_st1"), 16 * c.TOT_ST[1]), (("s_gmm"), c.BANDS_GMM)))
        for n in (1, 2):
            vstep(lambda e, tc=tc, n=n, xb=xb: e.scalar_tensor_tensor(
                c.xpre, xb(n), c.hpre[:, 4 * tc + n:4 * tc + n + 1], c.xpre,
                ALU.mult, ALU.add))
        vstep(lambda e, tc=tc, xb=xb: e.scalar_tensor_tensor(
            c.juggle[:, 1024 * tc:1024 * (tc + 1)], xb(3),
            c.hpre[:, 4 * tc + 3:4 * tc + 4], c.xpre, ALU.mult, ALU.add))
    XPRE = T.v("s_tv")

    def transpose_sweep(src, dst, gate):
        for tc in range(NTT):
            for gq in range(2):
                def mk_mm(e, q, bk, tc=tc, gq=gq, src=src):
                    cc = 4 * gq + q
                    return e.matmul(
                        c.bank(bk)[:, 128 * q:128 * (q + 1)],
                        src[:, 1024 * tc + 128 * cc:1024 * tc + 128 * (cc + 1)],
                        c.ident, start=True, stop=True, skip_group_check=True)
                def mk_ev(e, bk, tc=tc, gq=gq, dst=dst):
                    return e.activation(
                        dst.rearrange("p (c t) -> p c t", t=1024)
                           [:, 4 * gq:4 * gq + 4, 128 * tc:128 * (tc + 1)],
                        c.bank(bk).rearrange("p (c t) -> p c t", t=128),
                        AF.Identity)
                _mm_group(c, 4, mk_mm, "scalar", mk_ev, extra_mm_waits=gate)
        return gev_now(c)

    if _FFN_PART < 26:
        c.MIX = 0
        c.YTK_GATE = ()
        c.FFN_G = c.g_seq
        return
    XPT_GATE = transpose_sweep(c.juggle, c.xpreT, (("s_tv", XPRE),))
    if _FFN_PART < 255.5:
        c.MIX = 0
        c.YTK_GATE = ()
        c.FFN_G = c.g_seq
        return

    T.emit("gpsimd", lambda e: e.dma_start(out=c.wbr_bf, in_=c.wbr_d),
        waits=(("s_xn", c.P0_XN),), incs=(("s_wbr", 16),))
    WBR = 16
    if _FFN_PART < 256.5:
        c.MIX = 0
        c.YTK_GATE = ()
        c.FFN_G = c.g_seq
        return
    for occ in range(8):
        for tb in range(2):
            def mk_mm(e, i, bk, occ=occ, tb=tb):
                return e.matmul(
                    c.bank(bk), c.wbr_bf[:, 1024 * occ + 128 * i:1024 * occ + 128 * (i + 1)],
                    c.xpreT[:, 1024 * i + 512 * tb:1024 * i + 512 * (tb + 1)],
                    start=(i == 0), stop=(i == 7), skip_group_check=True)
            def mk_ev(e, bk, occ=occ, tb=tb):
                return e.activation(
                    c.juggle[:, 1024 * occ + 512 * tb:1024 * occ + 512 * (tb + 1)],
                    c.bank(bk), AF.Identity)
            _mm_group(c, 8, mk_mm, "scalar", mk_ev,
                      extra_mm_waits=(("s_wbr", WBR),) + XPT_GATE)
    YT_GATE = gev_now(c)

    if _FFN_PART < 27:
        c.MIX = 0
        c.YTK_GATE = ()
        c.FFN_G = c.g_seq
        return
    YTK_GATE = transpose_sweep(c.juggle, c.ytok, YT_GATE)
    c.YTK_GATE = YTK_GATE

    if _FFN_PART < 28:
        c.MIX = 0
        c.FFN_G = c.g_seq
        return

    for tc in range(NTT):
        xb = lambda n, tc=tc: c.bigB[:, D * tc + 1024 * n:D * tc + 1024 * (n + 1)]
        hcol = lambda n, j, tc=tc: c.hres[:, 16 * tc + 4 * n + j:16 * tc + 4 * n + j + 1]
        for n in range(4):
            vstep(lambda e, tc=tc, n=n, xb=xb, hcol=hcol: e.tensor_scalar_mul(
                c.xpre, xb(0), hcol(n, 0)), ((("s_tv"), HRES),) + YTK_GATE)
            for j in (1, 2, 3):
                vstep(lambda e, tc=tc, n=n, j=j, xb=xb, hcol=hcol: e.scalar_tensor_tensor(
                    c.xpre, xb(j), hcol(n, j), c.xpre, ALU.mult, ALU.add))
            vstep(lambda e, tc=tc, n=n, hcol=hcol: e.scalar_tensor_tensor(
                c.scrbf[:, 1024 * n:1024 * (n + 1)],
                c.ytok[:, 1024 * tc:1024 * (tc + 1)],
                c.hpost[:, 4 * tc + n:4 * tc + n + 1], c.xpre, ALU.mult, ALU.add))
        T.emit("vector", lambda e, tc=tc: e.tensor_copy(
            c.bigB[:, D * tc:D * (tc + 1)], c.scrbf),
            waits=(("s_tv", T.v("s_tv")),), incs=(("s_mix", 1),))
    c.MIX = T.v("s_mix")
    c.FFN_G = c.g_seq


def _emit_attention(c):
    T = c.T
    T.emit("gpsimd", lambda e: e.dma_start(out=c.ea_f, in_=c.ea_d),
           waits=(("s_gmm", T.v("s_gmm")),), incs=(("s_ea", 16),))
    EA = 16

    c.PREATT_GATE = gev_now(c)
    grp = 0
    qu = 0
    pe_grp_after_head = []
    kv_cnt = 0
    q_cnt = 0
    first = True
    for hi in range(H):
        h = hi
        par = hi % 2
        qs = QS_H[h]
        if hi >= 2:
            gate = [("s_ape", pe_grp_after_head[hi - 2])]
        else:
            gate = list(c.YTK_GATE) + [("s_mix", c.MIX)]
        gate_k = gate + [("s_akk", 16 * kv_cnt)]
        gate_v = gate + [("s_akv", 16 * kv_cnt)]
        for ck in range(32):
            r0 = (ck % 4) * 4096 + h * 128
            cl = (ck // 4) * 128
            T.emit("sync", lambda e, par=par, ck=ck, r0=r0, cl=cl: e.dma_start(
                out=c.kun[par][:, 128 * ck:128 * (ck + 1)],
                in_=c.cck_o.ap()[r0:r0 + 128, cl:cl + 128]),
                waits=tuple(gate_k) + (("s_ccK", 1),), incs=(("s_akk", 16),))
            vr = (ck % 4) * (H * 1024) + h * 1024 + (ck // 4) * 128
            T.emit("scalar", lambda e, par=par, ck=ck, vr=vr: e.dma_start(
                out=c.vun[par][:, 128 * ck:128 * (ck + 1)],
                in_=c.ccv_o.ap()[vr:vr + 128, :]),
                waits=tuple(gate_v) + (("s_ccV", 1),), incs=(("s_akv", 16),))
            gate_k, gate_v = [], []
        kv_cnt += 32
        T.emit("sync", lambda e, par=par, h=h: e.dma_start(
            out=c.qh[par], in_=c.qdr.ap()[h * 128:h * 128 + 128, :]),
            waits=(("s_st0", 16 * c.TOT_ST[0]), ("s_st1", 16 * c.TOT_ST[1]),
                   ("s_aq", 16 * q_cnt)),
            incs=(("s_aq", 16),))
        q_cnt += 1

        for t in range(NTT):
            for qb in range(NQB_H[h]):
                lo = c_lo(h, t)
                chunks = list(range(lo, 4 * t + 4))
                eoff = EA_OFF[(h, t, qb)]
                per_bank = 4 if qs == 128 else 8
                o_bk = 4 + (qu % 2)
                l_r0 = 64 * (qu % 2)
                ngr = (len(chunks) + per_bank - 1) // per_bank
                gfirst = grp
                for gi in range(ngr):
                    gch = chunks[gi * per_bank:(gi + 1) * per_bank]
                    sbk = grp % 2
                    for pi, ck in enumerate(gch):
                        T.emit("tensor",
                               lambda e, ck=ck, pi=pi, sbk=sbk, par=par, t=t, qb=qb, qs=qs, h=h:
                               e.matmul(
                                   c.bank(sbk)[:, qs * pi:qs * (pi + 1)],
                                   c.kun[par][:, 128 * ck:128 * (ck + 1)],
                                   c.qh[par][:, 128 * t + qs * qb:128 * t + qs * (qb + 1)],
                                   start=True, stop=True, skip_group_check=True),
                               waits=(("s_akk", 16 * kv_cnt), ("s_aq", 16 * q_cnt),
                                      ("s_aex", max(0, grp - 1))) + c.PREATT_GATE,
                               incs=(("s_asm", 1),) if pi == len(gch) - 1 else ())
                    SM = T.v("s_asm")
                    w = qs * len(gch)
                    T.emit("scalar", lambda e, sbk=sbk, w=w, par2=grp % 2: e.activation(
                        c.ering[par2][:, 0:w], c.bank(sbk, w=w), AF.Exp),
                        waits=(("s_asm", SM), ("s_apt", max(0, grp - 1)),
                               ("s_mix", c.MIX)),
                        incs=(("s_aex", 1),))
                    AEX = T.v("s_aex")
                    for pi, ck in enumerate(gch):
                        rel = ck - 4 * t
                        mo = mask_off(rel, qs, qb)
                        msk = c.onesb.broadcast_to([128, qs]) if mo is None \
                            else c.masks[:, mo:mo + qs]
                        T.emit("vector",
                               lambda e, pi=pi, ck=ck, par2=grp % 2, qs=qs,
                               eo=eoff + (ck - lo), msk=msk:
                               e.scalar_tensor_tensor(
                                   c.pring[par2][:, qs * pi:qs * (pi + 1)],
                                   c.ering[par2][:, qs * pi:qs * (pi + 1)],
                                   c.ea_f[:, eo:eo + 1], msk,
                                   ALU.mult, ALU.mult),
                               waits=(("s_aex", AEX), ("s_ea", EA),
                                      ("s_mk", 16), ("s_ape", max(0, grp - 1))),
                               incs=(("s_apt", 1),) if pi == len(gch) - 1 else ())
                    PT = T.v("s_apt")
                    for pi, ck in enumerate(gch):
                        cidx = gi * per_bank + pi
                        is_first = (cidx == 0)
                        is_last = (cidx == len(chunks) - 1)
                        T.emit("tensor",
                               lambda e, ck=ck, pi=pi, par=par, par2=grp % 2, qs=qs,
                               o_bk=o_bk, st=is_first, sp=is_last:
                               e.matmul(
                                   c.bank(o_bk, w=qs),
                                   c.vun[par][:, 128 * ck:128 * (ck + 1)],
                                   c.pring[par2][:, qs * pi:qs * (pi + 1)],
                                   start=st, stop=sp, skip_group_check=True),
                               waits=(("s_apt", PT), ("s_akv", 16 * kv_cnt),
                                      ("s_aoe", max(0, qu - 1)),))
                        T.emit("tensor",
                               lambda e, ck=ck, pi=pi, par2=grp % 2, qs=qs,
                               l_r0=l_r0, st=is_first, sp=is_last:
                               e.matmul(
                                   c.bank(6, w=qs, r0=l_r0, rn=1),
                                   c.onesb,
                                   c.pring[par2][:, qs * pi:qs * (pi + 1)],
                                   start=st, stop=sp, skip_group_check=True),
                               waits=(("s_alr", max(0, qu - 1)),),
                               incs=(("s_ape", 1),) if pi == len(gch) - 1 else ())
                    grp += 1
                APE = T.v("s_ape")
                T.emit("vector", lambda e, l_r0=l_r0, qs=qs: e.reciprocal(
                    c.rlq[0:1, 0:qs], c.bank(6, w=qs, r0=l_r0, rn=1)),
                    waits=(("s_ape", APE),), incs=(("s_alr", 1),))
                ALR = T.v("s_alr")
                T.emit("tensor", lambda e, qs=qs: e.matmul(
                    c.bank(7, w=qs), c.onesf, c.rlq[0:1, 0:qs],
                    start=True, stop=True, skip_group_check=True),
                    waits=(("s_alr", ALR), ("s_arb", max(0, qu))),
                    incs=(("s_arm", 1),))
                ARM = T.v("s_arm")
                T.emit("vector", lambda e, qs=qs: e.tensor_copy(
                    c.rbsb[:, 0:qs], c.bank(7, w=qs)),
                    waits=(("s_arm", ARM),), incs=(("s_arb", 1),))
                ARB = T.v("s_arb")
                dst0 = h * 1024 + 128 * t + qs * qb
                T.emit("vector", lambda e, o_bk=o_bk, qs=qs, dst0=dst0: e.tensor_tensor(
                    c.bigA[:, dst0:dst0 + qs], c.bank(o_bk, w=qs), c.rbsb[:, 0:qs],
                    ALU.mult),
                    waits=(("s_arb", ARB),),
                    incs=(("s_aoe", 1),))
                qu += 1
        pe_grp_after_head.append(grp)
    c.ATT_GRP = grp
    c.ATT_QU = qu
    c.APE_ALL = T.v("s_ape")
    c.AOE_ALL = T.v("s_aoe")
    c.AEX_ALL = T.v("s_aex")


def _emit_wout(c):
    T = c.T
    for j in range(8):
        T.emit("sync", lambda e, j=j: e.dma_start(out=c.rowb, in_=c.bo_d[j:j + 1, :]),
               waits=(("s_bob", j), ("s_ape", c.APE_ALL),
                      ("s_aoe", c.AOE_ALL), ("s_alr", c.ATT_QU)),
               incs=(("s_row", 16),))
        RW = T.v("s_row")
        T.emit("tensor", lambda e: e.matmul(
            c.bank(7), c.onesf, c.rowb, start=True, stop=True,
            skip_group_check=True),
            waits=(("s_row", RW), ("s_bob", j), ("s_arb", c.ATT_QU)),
            incs=(("s_bom", 1),))
        BM = T.v("s_bom")
        T.emit("vector", lambda e, j=j: e.tensor_copy(
            c.bob[:, 512 * j:512 * (j + 1)], c.bank(7)),
            waits=(("s_bom", BM),), incs=(("s_bob", 1),))
    BOB = T.v("s_bob")

    def vstep(fn, extra=()):
        T.emit("vector", fn, waits=tuple(extra) + (("s_tv", T.v("s_tv")),),
               incs=(("s_tv", 1),))

    seq = 0
    c.out_lane = [0, 0]
    c.ax_lane = [0, 0]
    groups_after_wb = []
    for wb in range(32):
        par = wb % 2
        if wb < 2:
            gate = [("s_ape", c.APE_ALL)]
        else:
            gate = [("s_gmm", groups_after_wb[wb - 2])]
        wsem = f"s_w{par}"
        T.emit("gpsimd", lambda e, par=par, wb=wb: e.dma_start(
            out=c.band[par], in_=c.wout_d[wb]),
            waits=tuple(gate), incs=((wsem, 16),))
        WB = T.v(wsem)
        for tt in range(NTT):
            def mk_mm(e, i, bk, par=par, tt=tt):
                return e.matmul(
                    c.bank(bk, w=128),
                    c.bigA[:, 1024 * i + 128 * tt:1024 * i + 128 * (tt + 1)],
                    c.band[par][:, 128 * i:128 * (i + 1)],
                    start=(i == 0), stop=(i == 31), skip_group_check=True)
            def mk_ev(e, bk, wb=wb):
                return e.tensor_tensor(
                    c.ot1, c.bank(bk, w=128), c.bob[:, 128 * wb:128 * (wb + 1)],
                    ALU.add)
            _mm_group(c, 32, mk_mm, "vector", mk_ev,
                      extra_mm_waits=((wsem, WB), ("s_aoe", c.AOE_ALL),
                                      ("s_aex", c.AEX_ALL)),
                      extra_evac_waits=(("s_bob", BOB),))
            GEV1 = T.v("s_gevv")
            axs = f"s_ax{seq % 2}"
            T.emit("sync", lambda e, wb=wb, tt=tt, seq=seq: e.dma_start(
                out=c.xcol[seq % 2],
                in_=c.x_d[128 * tt:128 * (tt + 1), 128 * wb:128 * (wb + 1)]),
                waits=(("s_wev", max(0, seq - 1)),), incs=((axs, 16),))
            AXV = T.v(axs)
            T.emit("vector", lambda e, seq=seq: e.tensor_add(
                c.ot2, c.ot1, c.xcol[seq % 2]),
                waits=((axs, AXV), ("s_gevv", GEV1)),
                incs=(("s_wev", 1),))
            WEV1 = T.v("s_wev")
            T.emit("vector", lambda e, seq=seq, wb=wb, tt=tt: e.tensor_tensor(
                c.ostg[seq % 2], c.ot2,
                c.bigB[:, D * tt + 128 * wb:D * tt + 128 * (wb + 1)], ALU.add),
                waits=(("s_wev", WEV1),
                       (f"s_out{seq % 2}", 16 * c.out_lane[seq % 2]),
                       ("s_mix", c.MIX)),
                incs=(("s_w3", 1),))
            W3 = T.v("s_w3")
            T.emit("scalar", lambda e, seq=seq, wb=wb, tt=tt: e.dma_start(
                out=c.out_d[128 * tt:128 * (tt + 1), 128 * wb:128 * (wb + 1)],
                in_=c.ostg[seq % 2]),
                waits=(("s_w3", W3),), incs=((f"s_out{seq % 2}", 16),))
            c.out_lane[seq % 2] += 1
            seq += 1
        groups_after_wb.append(c.g_seq)


def _emit_debug(c):
    T = c.T
    if hasattr(c, "TOT_ST"):
        stw = (("s_st0", 16 * c.TOT_ST[0]), ("s_st1", 16 * c.TOT_ST[1]))
        T.emit("sync", lambda e: e.dma_start(out=c.dbg_q, in_=c.qdr.ap()),
               waits=stw, incs=(("s_outd", 16),))
        T.emit("sync", lambda e: e.dma_start(out=c.dbg_k, in_=c.cck_i.ap()),
               waits=stw, incs=(("s_outd", 16),))
        T.emit("sync", lambda e: e.dma_start(out=c.dbg_v, in_=c.ccv_i.ap()),
               waits=stw, incs=(("s_outd", 16),))
    T.emit("sync", lambda e: e.dma_start(out=c.dbg_bigB, in_=c.bigB),
           waits=(("s_xn", T.v("s_xn")), ("s_mix", T.v("s_mix"))),
           incs=(("s_outd", 16),))
    T.emit("sync", lambda e: e.dma_start(out=c.dbg_bigA, in_=c.bigA),
           waits=(("s_tev", T.v("s_tev")), ("s_aoe", T.v("s_aoe"))),
           incs=(("s_outd", 16),))
    T.emit("sync", lambda e: e.dma_start(out=c.dbg_f32[:, 0:192], in_=c.lgraw),
           waits=getattr(c, "LGR_GATE", ()), incs=(("s_outd", 16),))
    T.emit("sync", lambda e: e.dma_start(out=c.dbg_f32[:, 192:384], in_=c.lgs),
           incs=(("s_outd", 16),))
    T.emit("sync", lambda e: e.dma_start(out=c.dbg_f32[:, 384:512], in_=c.hres),
           waits=(("s_tv", T.v("s_tv")),), incs=(("s_outd", 16),))
    T.emit("sync", lambda e: e.dma_start(out=c.dbg_f32[:, 512:544], in_=c.hpre),
           incs=(("s_outd", 16),))
    T.emit("sync", lambda e: e.dma_start(out=c.dbg_f32[:, 544:576], in_=c.hpost),
           incs=(("s_outd", 16),))
    T.emit("sync", lambda e: e.dma_start(out=c.dbg_f32[:, 576:584], in_=c.rms_r),
           incs=(("s_outd", 16),))


def kernel(**inputs):
    shared, per_core = prep_inputs(inputs)
    nc = build_program()
    in_maps = [dict(shared, **pc) for pc in per_core]
    res = run_bass_kernel_spmd(nc, in_maps, core_ids=list(range(N_CORES)))
    out = np.zeros((B, S, D), dtype=np.float32)
    for cc in range(N_CORES):
        b, ci = cc // 4, cc % 4
        oc = res.results[cc]["out_core"]
        for t in range(NTT):
            out[b, 128 * (4 * t + ci):128 * (4 * t + ci) + 128, :] = \
                oc[128 * t:128 * (t + 1), :]
    return out
